# version 03 (final)
# baseline (speedup 1.0000x reference)
import numpy as np
from contextlib import ExitStack

import concourse.bass as bass
import concourse.mybir as mybir
from concourse.bass_utils import run_bass_kernel_spmd

F32 = mybir.dt.float32
BF16 = mybir.dt.bfloat16
AF = mybir.ActivationFunctionType
OP = mybir.AluOpType
AX = mybir.AxisListType

D = 1024
NST = 16
RDT = 64
E = 8
TB = 8192
TQ = 2048
CH = 256
NBLK = 8
TA = 1024
NCHUNK = TB // TA
TC = 256
NSUB = TA // TC
LN_EPS = 1e-5

SEMS = ("qg", "qs", "sv", "sk", "st", "cc", "sg")
ENGINES = ("tensor", "vector", "scalar", "gpsimd", "sync")


class Em:

    def __init__(self, name, eng, sems):
        self.name = name
        self.eng = eng
        self.sems = sems
        self.c = {k: 0 for k in sems}
        self._q = None

    def inc(self, s, n):
        self.c[s] += n
        return self.c[s]

    def dma(self, qname, sem, fn):
        v = self.inc(sem, 16)
        if self.name == qname:
            out, in_ = fn(self.eng)
            self.eng.dma_start(out=out, in_=in_).then_inc(self.sems[sem], 16)
        return v

    def op(self, ename, fn, sem=None):
        v = self.inc(sem, 1) if sem else None
        if self.name == ename:
            ins = fn(self.eng)
            if sem:
                ins.then_inc(self.sems[sem], 1)
        return v

    def wait(self, ename, sem, val):
        if self.name == ename and val and val > 0:
            self.eng.wait_ge(self.sems[sem], val)

    def q(self):
        if self._q is None:
            pid = self.eng.partition_id()
            self._q = pid % 4
        return self._q


def build(stage="full", cut=99, experts=None, no_comb=False, no_down=False):
    nc = bass.Bass("TRN2", target_bir_lowering=False, debug=False)

    xf = nc.dram_tensor("xf", [D, TB], F32, kind="ExternalInput").ap()
    xo = nc.dram_tensor("xo", [CH, TB], F32, kind="ExternalInput").ap()
    wbN = nc.dram_tensor("wbN", [128, NBLK * NST], F32, kind="ExternalInput").ap()
    wcN = nc.dram_tensor("wcN", [128, NBLK * NST], F32, kind="ExternalInput").ap()
    w1N = nc.dram_tensor("w1N", [128, NBLK * RDT], F32, kind="ExternalInput").ap()
    w2N = nc.dram_tensor("w2N", [RDT, CH], F32, kind="ExternalInput").ap()
    bdN = nc.dram_tensor("bdN", [128, 2], F32, kind="ExternalInput").ap()
    laN = nc.dram_tensor("laN", [128, 2 * NST], F32, kind="ExternalInput").ap()
    dsN = nc.dram_tensor("dsN", [128, 2], F32, kind="ExternalInput").ap()
    gmN = nc.dram_tensor("gmN", [128, NBLK], F32, kind="ExternalInput").ap()
    btN = nc.dram_tensor("btN", [128, NBLK], F32, kind="ExternalInput").ap()
    rtN = nc.dram_tensor("rtN", [128, NBLK * E], F32, kind="ExternalInput").ap()
    eyN = nc.dram_tensor("eyN", [128, 128], F32, kind="ExternalInput").ap()
    ctN = nc.dram_tensor("ctN", [128, 4], F32, kind="ExternalInput").ap()
    guN = nc.dram_tensor("guN", [E, 48, 128, NBLK * 128], F32, kind="ExternalInput").ap()
    dnN = nc.dram_tensor("dnN", [E, 8, 128, 24 * 128], F32, kind="ExternalInput").ap()
    sgN = nc.dram_tensor("sgN", [16, 128, NBLK * 128], F32, kind="ExternalInput").ap()
    sdN = nc.dram_tensor("sdN", [8, 128, NBLK * 128], F32, kind="ExternalInput").ap()

    stath = nc.dram_tensor("stath", [2, TB], F32).ap()
    bch = nc.dram_tensor("bch", [2, TB // TC, NST * TC], F32).ap()
    combh = nc.dram_tensor("combh", [E, TQ], F32).ap()
    statc = nc.dram_tensor("statc", [2, TQ], F32).ap()
    if stage == "A":
        sendu = nc.dram_tensor("udev", [CH, TB], F32, kind="ExternalOutput").ap()
        dbg_dt = nc.dram_tensor("dbg_dt", [128, 2, TA], F32, kind="ExternalOutput").ap()
        dbg_xln = nc.dram_tensor("dbg_xln", [128, 2, TA], F32, kind="ExternalOutput").ap()
        dbg_rs = nc.dram_tensor("dbg_rs", [128, TA], F32, kind="ExternalOutput").ap()
        dbg_mu = nc.dram_tensor("dbg_mu", [128, TA], F32, kind="ExternalOutput").ap()
        dbg_B = nc.dram_tensor("dbg_B", [NST, TA], F32, kind="ExternalOutput").ap()
        dbg_ab = nc.dram_tensor("dbg_ab", [128, NST * TC], F32, kind="ExternalOutput").ap()
        dbg_bb = nc.dram_tensor("dbg_bb", [128, NST * TC], F32, kind="ExternalOutput").ap()
        dbg_hh = nc.dram_tensor("dbg_hh", [128, NST * TC], F32, kind="ExternalOutput").ap()
        dbg_A = nc.dram_tensor("dbg_A", [128, 2 * NST], F32, kind="ExternalOutput").ap()
        dbg_hs = nc.dram_tensor("dbg_hs", [128, 2 * NST], F32, kind="ExternalOutput").ap()
        dbg_rA = nc.dram_tensor("dbg_rA", [128, 2 * NST], F32, kind="ExternalOutput").ap()
        dbg_Cb = nc.dram_tensor("dbg_Cb", [128, NST * TC], F32, kind="ExternalOutput").ap()
        dbg_C = nc.dram_tensor("dbg_C", [NST, TA], F32, kind="ExternalOutput").ap()
        dbg_y = nc.dram_tensor("dbg_y", [128, TC], F32, kind="ExternalOutput").ap()
        dbg_hc = nc.dram_tensor("dbg_hc", [128, NST * TC], F32, kind="ExternalOutput").ap()
        recvu, out = None, None
    else:
        sendu = nc.dram_tensor("sendu", [CH, TB], F32).ap()
        recvu = nc.dram_tensor("recvu", [D, TB], F32).ap()
        out = nc.dram_tensor("out", [D, TQ], F32, kind="ExternalOutput").ap()

    xf_r = xf.rearrange("(b p) t -> p b t", p=128)
    xo_r = xo.rearrange("(b p) t -> p b t", p=128)
    su_r = sendu.rearrange("(b p) t -> p b t", p=128)
    if stage != "A":
        rv_r = recvu.rearrange("(b p) t -> p b t", p=128)
        out_r = out.rearrange("(b p) t -> p b t", p=128)

    al = [16512]

    offs = {}

    def sb(name, shape, dtype):
        nbytes = int(np.prod(shape[1:])) * mybir.dt.size(dtype)
        off = al[0]
        offs[name] = off
        al[0] = off + ((nbytes + 63) // 64) * 64
        return nc.alloc_sbuf_tensor_at(name, list(shape), dtype, offset=off).ap()

    s_ones = sb("s_ones", [128, 1], BF16)
    s_eye = sb("s_eye", [128, 128], F32)
    s_ct = sb("s_ct", [128, 4], F32)
    s_wb = sb("s_wb", [128, NBLK * NST], BF16)
    s_wc = sb("s_wc", [128, NBLK * NST], BF16)
    s_w1 = sb("s_w1", [128, NBLK * RDT], BF16)
    s_w2 = sb("s_w2", [RDT, CH], BF16)
    s_rt = sb("s_rt", [128, NBLK * E], BF16)
    s_gm = sb("s_gm", [128, NBLK], F32)
    s_bt = sb("s_bt", [128, NBLK], F32)
    s_A = sb("s_A", [128, 2 * NST], F32)
    s_rA = sb("s_rA", [128, 2 * NST], F32)
    s_ds = sb("s_ds", [128, 2], F32)
    s_bd = sb("s_bd", [128, 2], F32)
    s_nwB = sb("s_nwB", [NST, 1], F32)
    s_nwC = sb("s_nwC", [NST, 1], F32)
    s_nw1 = sb("s_nw1", [RDT, 1], F32)
    arena = al[0]

    a_xf = sb("a_xf", [128, 2, NBLK, TA], BF16)
    a_xo = sb("a_xo", [128, 2, TA], F32)
    a_sq = sb("a_sq", [128, 2, TA], BF16)
    a_mu = sb("a_mu", [128, TA], F32)
    a_rs = sb("a_rs", [128, TA], F32)
    a_xln = sb("a_xln", [128, 2, TA], F32)
    a_dt = sb("a_dt", [128, 2, TA], F32)
    a_Bf = sb("a_Bf", [NST, TA], F32)
    a_Cf = sb("a_Cf", [NST, TA], F32)
    a_d1 = sb("a_d1", [RDT, TA], BF16)
    a_ab = sb("a_ab", [128, NST * TC], F32)
    a_b0 = sb("a_b0", [128, NST * TC], F32)
    a_bb = sb("a_bb", [128, NST * TC], F32)
    a_hh = sb("a_hh", [128, NST * TC], F32)
    a_Bb = sb("a_Bb", [128, 2, NST * TC], BF16)
    a_Cb = sb("a_Cb", [128, 2, NST * TC], BF16)
    a_hs = sb("a_hs", [128, 2 * NST], F32)
    a_u = sb("a_u", [128, 2, 2, TA], F32)
    a_row = sb("a_row", [1, 2, TA], F32)
    _row_off = offs["a_row"]
    a_y = nc.alloc_sbuf_tensor_at("a_y", [128, TC], F32, offset=_row_off).ap()
    a_sp = nc.alloc_sbuf_tensor_at("a_sp", [128, 512], F32,
                                   offset=_row_off + TC * 4).ap()

    def arow(r, sl=slice(None)):
        return a_row[0:1, r, sl]

    a_end = al[0]
    assert a_end <= 229312, f"phase A sbuf overflow: {a_end}"

    al[0] = arena
    c_u = sb("c_u", [128, NBLK, TQ], F32)
    c_nb = sb("c_nb", [128, NBLK, TQ], BF16)
    c_hb = sb("c_hb", [128, 24, 1024], BF16)
    c_mu = sb("c_mu", [128, TQ], BF16)
    c_al = sb("c_al", [128, TQ], BF16)
    persist = al[0]
    c_ub = sb("c_ub", [128, 2, TQ], BF16)
    c_sq = sb("c_sq", [128, 2, TQ], BF16)
    c_row = nc.alloc_sbuf_tensor_at("c_row", [1, 2, TQ], F32,
                                    offset=offs["c_hb"]).ap()
    c_v2 = nc.alloc_sbuf_tensor_at("c_v2", [128, TQ], F32,
                                   offset=offs["c_hb"] + 2 * TQ * 4).ap()

    def crow(r, sl=slice(None)):
        return c_row[0:1, r, sl]

    c_lg = sb("c_lg", [8, TQ], F32)
    c_t32 = sb("c_t32", [128, TQ], F32)
    c_lgT = sb("c_lgT", [128, 16 * E], F32)
    c_ket = sb("c_ket", [128, 16 * E], F32)
    c_m1 = sb("c_m1", [128, 16 * E], F32)
    c_md = sb("c_md", [128, 16 * E], F32)
    c_sc = sb("c_sc", [128, 7 * 16], F32)
    c_cmb = sb("c_cmb", [128, 16 * E], F32)
    c_cT = None
    pro_end = al[0]
    al[0] = persist
    c_cb = sb("c_cb", [128, TQ], BF16)
    c_wg = sb("c_wg", [128, 2, NBLK * 128], BF16)
    c_wu = sb("c_wu", [128, 2, NBLK * 128], BF16)
    c_wd = sb("c_wd", [128, 2, 24 * 128], BF16)
    c_sg = sb("c_sg", [128, 2, 1024], BF16)
    c_uc = sb("c_uc", [128, 1024], BF16)
    loop_end = al[0]
    assert max(pro_end, loop_end) <= 229312, f"phase C sbuf overflow {pro_end}/{loop_end}"

    ps = [nc.alloc_psum_tensor(f"ps{i}", [128, 512], F32).ap() for i in range(8)]

    with ExitStack() as stk:
        sems = {s: stk.enter_context(nc.semaphore(s)) for s in SEMS}
        block = stk.enter_context(nc.Block())

        def program(em):
            em.dma("gpsimd", "qg", lambda e: (s_wb[:, :], wbN[:, :]))
            em.dma("gpsimd", "qg", lambda e: (s_wc[:, :], wcN[:, :]))
            em.dma("gpsimd", "qg", lambda e: (s_w1[:, :], w1N[:, :]))
            em.dma("gpsimd", "qg", lambda e: (s_w2[:, :], w2N[:, :]))
            v_small = em.dma("gpsimd", "qg", lambda e: (s_rt[:, :], rtN[:, :]))
            em.dma("sync", "qs", lambda e: (s_gm[:, :], gmN[:, :]))
            em.dma("sync", "qs", lambda e: (s_bt[:, :], btN[:, :]))
            em.dma("sync", "qs", lambda e: (s_A[:, :], laN[:, :]))
            em.dma("sync", "qs", lambda e: (s_ds[:, :], dsN[:, :]))
            em.dma("sync", "qs", lambda e: (s_eye[:, :], eyN[:, :]))
            em.dma("sync", "qs", lambda e: (s_ct[:, :], ctN[:, :]))
            v_f32 = em.dma("sync", "qs", lambda e: (s_bd[:, :], bdN[:, :]))
            v_xf = [0] * (NCHUNK + 1)
            v_xo = [0] * (NCHUNK + 1)
            v_xf[0] = em.dma("gpsimd", "qg",
                             lambda e: (a_xf[:, 0], xf_r[:, :, 0:TA]))

            em.op("vector", lambda e: e.memset(s_ones[:, :], 1.0))
            em.wait("scalar", "qs", v_f32)
            em.op("scalar", lambda e: e.activation(out=s_A[:, :], in_=s_A[:, :],
                                                   func=AF.Exp), sem="sk")
            v_A0 = em.c["sk"]
            em.wait("vector", "sk", v_A0)
            em.op("vector", lambda e: e.tensor_scalar_mul(s_A[:, :], s_A[:, :], -1.0),
                  sem="sv")
            em.wait("vector", "sv", em.c["sv"])
            em.op("vector", lambda e: e.tensor_scalar_add(a_hs[:, :], s_A[:, :], 1e-8),
                  sem="sv")
            em.wait("vector", "sv", em.c["sv"])
            em.op("vector", lambda e: e.reciprocal(s_rA[:, :], a_hs[:, :]), sem="sv")
            v_A = em.c["sv"]

            em.wait("tensor", "qg", v_small)
            em.wait("tensor", "sv", v_A)

            def wsum(dst, w, m):
                for k in range(NBLK):
                    em.op("tensor", lambda e, k=k: e.matmul(
                        dst, w[:, k * m:(k + 1) * m], s_ones[:, :],
                        start=(k == 0), stop=(k == NBLK - 1)),
                        sem="st" if k == NBLK - 1 else None)
            wsum(ps[0][0:NST, 0:1], s_wb, NST)
            wsum(ps[1][0:NST, 0:1], s_wc, NST)
            wsum(ps[2][0:RDT, 0:1], s_w1, RDT)
            v_ws = em.c["st"]
            em.wait("vector", "st", v_ws)
            em.op("vector", lambda e: e.tensor_scalar_mul(s_nwB[:, :], ps[0][0:NST, 0:1], -1.0))
            em.op("vector", lambda e: e.tensor_scalar_mul(s_nwC[:, :], ps[1][0:NST, 0:1], -1.0))
            em.op("vector", lambda e: e.tensor_scalar_mul(s_nw1[:, :], ps[2][0:RDT, 0:1], -1.0),
                  sem="sv")
            v_setup = em.c["sv"]
            em.wait("vector", "sv", v_setup)
            em.wait("scalar", "sv", v_setup)

            v_scan_done = [0, 0]
            v_hc_prev = 0
            v_u_prev = 0
            v_bc_used_par = {0: 0, 1: 0}
            v_bc_ld = {0: 0, 1: 0}
            v_sq_done = [0, 0]
            v_xf_free = [0, 0]
            v_ustore = [0] * (NCHUNK + 2)
            v_bc_used = 0
            v_abuf_free = 0
            pv_dt = pv_bcfix = pv_dtraw = pv_strow = pv_bcrows = pv_xln = 0
            pv_d1 = v_setup
            sq_free = {0: 0, 1: 0}

            for c in range(NCHUNK):
                par = c % 2
                em.wait("sync", "sv", v_scan_done[1 - par])
                v_xo[c] = em.dma("sync", "qs", lambda e, c=c: (
                    a_xo[:, :, :], xo_r[:, :, c * TA:(c + 1) * TA]))
                if c + 1 < NCHUNK:
                    em.wait("gpsimd", "st", v_xf_free[1 - par])
                    em.wait("gpsimd", "sk", v_sq_done[1 - par])
                    v_xf[c + 1] = em.dma("gpsimd", "qg", lambda e, c=c, par=par: (
                        a_xf[:, 1 - par], xf_r[:, :, (c + 1) * TA:(c + 2) * TA]))

                em.wait("scalar", "qg", v_xf[c])
                em.wait("tensor", "qg", v_xf[c])
                em.wait("tensor", "sv", pv_d1)
                em.wait("tensor", "sk", pv_dt)
                for blk in range(NBLK):
                    em.wait("scalar", "st", sq_free[blk % 2])
                    em.op("scalar", lambda e, blk=blk, par=par: e.activation(
                        out=a_sq[:, blk % 2], in_=a_xf[:, par, blk], func=AF.Square),
                        sem="sk")
                    v_sqb = em.c["sk"]
                    for tt in range(2):
                        em.op("tensor", lambda e, blk=blk, par=par, tt=tt: e.matmul(
                            ps[tt][0:1, :], s_ones[:, :],
                            a_xf[:, par, blk, tt * 512:(tt + 1) * 512],
                            start=(blk == 0), stop=(blk == NBLK - 1)))
                    em.wait("tensor", "sk", v_sqb)
                    for tt in range(2):
                        em.op("tensor", lambda e, blk=blk, tt=tt: e.matmul(
                            ps[2 + tt][0:1, :], s_ones[:, :],
                            a_sq[:, blk % 2, tt * 512:(tt + 1) * 512],
                            start=(blk == 0), stop=(blk == NBLK - 1)),
                            sem="st")
                    sq_free[blk % 2] = em.c["st"]
                v_sq_done[par] = em.c["sk"]
                v_stats_mm = em.c["st"]

                for blk in range(NBLK):
                    for tt in range(2):
                        em.op("tensor", lambda e, blk=blk, par=par, tt=tt: e.matmul(
                            ps[4 + tt][0:NST, :], s_wb[:, blk * NST:(blk + 1) * NST],
                            a_xf[:, par, blk, tt * 512:(tt + 1) * 512],
                            start=(blk == 0), stop=(blk == NBLK - 1)))
                        em.op("tensor", lambda e, blk=blk, par=par, tt=tt: e.matmul(
                            ps[6 + tt][0:NST, :], s_wc[:, blk * NST:(blk + 1) * NST],
                            a_xf[:, par, blk, tt * 512:(tt + 1) * 512],
                            start=(blk == 0), stop=(blk == NBLK - 1)),
                            sem="st" if (blk == NBLK - 1 and tt == 1) else None)
                v_bc_mm = em.c["st"]

                em.wait("vector", "st", v_stats_mm)
                em.wait("vector", "qs", pv_strow)
                for tt in range(2):
                    em.op("vector", lambda e, tt=tt: e.tensor_scalar_mul(
                        arow(0, slice(tt * 512, (tt + 1) * 512)), ps[tt][0:1, :], 1.0 / D))
                    em.op("vector", lambda e, tt=tt: e.tensor_scalar_mul(
                        arow(1, slice(tt * 512, (tt + 1) * 512)),
                        ps[2 + tt][0:1, :], 1.0 / D),
                        sem="sv" if tt == 1 else None)
                v_evac = em.c["sv"]
                em.wait("sync", "sv", v_evac)
                em.dma("sync", "qs", lambda e, c=c: (
                    stath[0:1, c * TA:(c + 1) * TA], arow(0)))
                v_strow = em.dma("sync", "qs", lambda e, c=c: (
                    stath[1:2, c * TA:(c + 1) * TA], arow(1)))
                em.wait("gpsimd", "qs", v_strow)
                em.wait("gpsimd", "sv", pv_xln)
                em.wait("gpsimd", "sk", pv_dt)
                em.dma("gpsimd", "qg", lambda e, c=c: (
                    a_mu[:, :],
                    stath[0, c * TA:(c + 1) * TA].unsqueeze(0).partition_broadcast(128)))
                v_bcast = em.dma("gpsimd", "qg", lambda e, c=c: (
                    a_rs[:, :],
                    stath[1, c * TA:(c + 1) * TA].unsqueeze(0).partition_broadcast(128)))
                em.wait("vector", "qg", v_bcast)
                em.op("vector", lambda e: e.tensor_tensor(
                    a_xln[:, 0], a_mu[:, :], a_mu[:, :], OP.mult))
                em.op("vector", lambda e: e.tensor_sub(
                    a_rs[:, :], a_rs[:, :], a_xln[:, 0]), sem="sv")
                v_var = em.c["sv"]
                em.wait("scalar", "sv", v_var)
                em.op("scalar", lambda e: e.activation(
                    out=a_xln[:, 0], in_=a_rs[:, :], func=AF.Ln, bias=s_ct[:, 3:4]))
                em.op("scalar", lambda e: e.activation(
                    out=a_rs[:, :], in_=a_xln[:, 0], func=AF.Exp, scale=s_ct[:, 0:1]),
                    sem="sk")
                v_rstd = em.c["sk"]

                em.wait("vector", "sk", v_rstd)
                em.wait("vector", "st", v_bc_mm)
                em.wait("vector", "qs", pv_bcrows)
                for tt in range(2):
                    sl = slice(tt * 512, (tt + 1) * 512)
                    em.op("vector", lambda e, tt=tt, sl=sl: e.scalar_tensor_tensor(
                        a_Bf[:, sl], a_mu[0:NST, sl], s_nwB[:, 0:1],
                        ps[4 + tt][0:NST, :], OP.mult, OP.add))
                    em.op("vector", lambda e, tt=tt, sl=sl: e.tensor_tensor(
                        a_Bf[:, sl], a_Bf[:, sl], a_rs[0:NST, sl], OP.mult))
                    em.op("vector", lambda e, tt=tt, sl=sl: e.scalar_tensor_tensor(
                        a_Cf[:, sl], a_mu[0:NST, sl], s_nwC[:, 0:1],
                        ps[6 + tt][0:NST, :], OP.mult, OP.add))
                    em.op("vector", lambda e, tt=tt, sl=sl: e.tensor_tensor(
                        a_Cf[:, sl], a_Cf[:, sl], a_rs[0:NST, sl], OP.mult),
                        sem="sv" if tt == 1 else None)
                v_bcfix = em.c["sv"]
                em.wait("sync", "sv", v_bcfix)
                em.dma("sync", "qs", lambda e, c=c: (
                    bch[0][c * NSUB:(c + 1) * NSUB].rearrange(
                        "s (n t) -> n s t", n=NST),
                    a_Bf.rearrange("n (s t) -> n s t", s=NSUB)))
                v_bcrows = em.dma("sync", "qs", lambda e, c=c: (
                    bch[1][c * NSUB:(c + 1) * NSUB].rearrange(
                        "s (n t) -> n s t", n=NST),
                    a_Cf.rearrange("n (s t) -> n s t", s=NSUB)))

                em.wait("tensor", "sv", v_evac)
                for blk in range(NBLK):
                    for tt in range(2):
                        em.op("tensor", lambda e, blk=blk, par=par, tt=tt: e.matmul(
                            ps[tt][0:RDT, :], s_w1[:, blk * RDT:(blk + 1) * RDT],
                            a_xf[:, par, blk, tt * 512:(tt + 1) * 512],
                            start=(blk == 0), stop=(blk == NBLK - 1)),
                            sem="st" if (blk == NBLK - 1 and tt == 1) else None)
                v_d1_mm = em.c["st"]
                v_xf_free[par] = v_d1_mm

                em.wait("vector", "st", v_d1_mm)
                em.wait("vector", "st", pv_dtraw)
                for tt in range(2):
                    sl = slice(tt * 512, (tt + 1) * 512)
                    em.op("vector", lambda e, tt=tt, sl=sl: e.scalar_tensor_tensor(
                        a_sp[0:RDT, 0:512], a_mu[0:RDT, sl], s_nw1[:, 0:1],
                        ps[tt][0:RDT, :], OP.mult, OP.add))
                    em.op("vector", lambda e, tt=tt, sl=sl: e.tensor_tensor(
                        a_d1[:, sl], a_sp[0:RDT, 0:512], a_rs[0:RDT, sl], OP.mult),
                        sem="sv" if tt == 1 else None)
                v_d1 = em.c["sv"]

                em.wait("tensor", "sv", v_d1)
                for blk in range(2):
                    for tt in range(2):
                        em.op("tensor", lambda e, blk=blk, tt=tt: e.matmul(
                            ps[2 + 2 * blk + tt][:, :],
                            s_w2[:, blk * 128:(blk + 1) * 128],
                            a_d1[:, tt * 512:(tt + 1) * 512],
                            start=True, stop=True),
                            sem="st" if (blk == 1 and tt == 1) else None)
                v_dtraw = em.c["st"]
                em.wait("scalar", "st", v_dtraw)
                for blk in range(2):
                    for tt in range(2):
                        em.op("scalar", lambda e, blk=blk, tt=tt: e.activation(
                            out=a_sp[:, :], in_=ps[2 + 2 * blk + tt][:, :],
                            func=AF.Exp, bias=s_bd[:, blk:blk + 1]))
                        em.op("scalar", lambda e, blk=blk, tt=tt: e.activation(
                            out=a_dt[:, blk, tt * 512:(tt + 1) * 512], in_=a_sp[:, :],
                            func=AF.Ln, bias=s_ct[:, 1:2]),
                            sem="sk" if (blk == 1 and tt == 1) else None)
                v_dt = em.c["sk"]

                em.wait("vector", "sk", v_rstd)
                em.wait("vector", "qs", v_xo[c])
                for blk in range(2):
                    em.op("vector", lambda e, blk=blk, par=par: e.tensor_sub(
                        a_xln[:, blk], a_xo[:, blk], a_mu[:, :]))
                    em.op("vector", lambda e, blk=blk: e.tensor_tensor(
                        a_xln[:, blk], a_xln[:, blk], a_rs[:, :], OP.mult),
                        sem="sv" if blk == 1 else None)
                v_xln = em.c["sv"]

                def bc_load(cs, pa):
                    em.wait("gpsimd", "qs", v_bcrows)
                    em.wait("gpsimd", "sv", v_bc_used_par[pa])
                    em.dma("gpsimd", "qg", lambda e, cs=cs, pa=pa: (
                        a_Bb[:, pa], bch[0, cs].unsqueeze(0).partition_broadcast(128)))
                    return em.dma("gpsimd", "qg", lambda e, cs=cs, pa=pa: (
                        a_Cb[:, pa], bch[1, cs].unsqueeze(0).partition_broadcast(128)))

                for s in range(NSUB):
                    t0 = s * TC
                    pa = s % 2
                    if s == 0:
                        v_bcbc = bc_load(c * NSUB, 0)
                        v_bc_ld[0] = v_bcbc
                    else:
                        v_bcbc = v_bc_ld[pa]
                    if s + 1 < NSUB:
                        v_bc_ld[1 - pa] = bc_load(c * NSUB + s + 1, 1 - pa)
                    for blk in range(2):
                        em.wait("scalar", "sk", v_dt)
                        em.wait("scalar", "sv", v_abuf_free)
                        for n in range(NST):
                            em.op("scalar", lambda e, blk=blk, n=n, t0=t0: e.activation(
                                out=a_ab[:, n * TC:(n + 1) * TC],
                                in_=a_dt[:, blk, t0:t0 + TC], func=AF.Exp,
                                scale=s_A[:, blk * NST + n:blk * NST + n + 1]),
                                sem="sk" if n == NST - 1 else None)
                        v_exp = em.c["sk"]

                        b03 = a_b0.rearrange("p (n t) -> p n t", n=NST)
                        em.op("vector", lambda e, blk=blk, t0=t0, b03=b03: e.tensor_tensor(
                            b03,
                            a_xln[:, blk, t0:t0 + TC].unsqueeze(1)
                            .broadcast_to([128, NST, TC]),
                            s_rA[:, blk * NST:(blk + 1) * NST].unsqueeze(2)
                            .broadcast_to([128, NST, TC]),
                            OP.mult))
                        em.wait("vector", "qg", v_bcbc)
                        em.op("vector", lambda e, pa=pa: e.tensor_tensor(
                            a_b0[:, :], a_b0[:, :], a_Bb[:, pa], OP.mult), sem="sv")
                        em.wait("vector", "sv", em.c["sv"])
                        em.wait("vector", "sk", v_exp)
                        em.op("vector", lambda e: e.scalar_tensor_tensor(
                            a_bb[:, :], a_ab[:, :], 1.0, a_b0[:, :],
                            OP.subtract, OP.mult))
                        first = (c == 0 and s == 0)
                        for n in range(NST):
                            init = 0.0 if first else a_hs[:, blk * NST + n:blk * NST + n + 1]
                            em.op("vector", lambda e, n=n, init=init: e.tensor_tensor_scan(
                                a_hh[:, n * TC:(n + 1) * TC],
                                a_ab[:, n * TC:(n + 1) * TC],
                                a_bb[:, n * TC:(n + 1) * TC],
                                init, OP.mult, OP.add),
                                sem="sv" if n == NST - 1 else None)
                        v_abuf_free = em.c["sv"]
                        em.wait("vector", "sv", v_abuf_free)
                        hh3 = a_hh.rearrange("p (n t) -> p n t", n=NST)
                        em.op("vector", lambda e, blk=blk, hh3=hh3: e.tensor_copy(
                            a_hs[:, blk * NST:(blk + 1) * NST], hh3[:, :, TC - 1]),
                            sem="sv")
                        v_hs = em.c["sv"]
                        em.wait("vector", "sv", v_hs)
                        em.op("vector", lambda e, pa=pa: e.tensor_tensor(
                            a_b0[:, :], a_hh[:, :], a_Cb[:, pa], OP.mult), sem="sv")
                        em.wait("vector", "sv", em.c["sv"])
                        hcT = a_b0.rearrange("p (n t) -> p n t", n=NST).transpose([0, 2, 1])
                        em.op("vector", lambda e, hcT=hcT: e.tensor_reduce(
                            a_y[:, :].unsqueeze(2), hcT, AX.X, OP.add))
                        em.wait("vector", "qs", v_ustore[c])
                        em.op("vector", lambda e, blk=blk, par=par, t0=t0:
                              e.scalar_tensor_tensor(
                                  a_u[:, par, blk, t0:t0 + TC],
                                  a_xln[:, blk, t0:t0 + TC], s_ds[:, blk:blk + 1],
                                  a_xo[:, blk, t0:t0 + TC], OP.mult, OP.add))
                        em.op("vector", lambda e, blk=blk, par=par, t0=t0: e.tensor_add(
                            a_u[:, par, blk, t0:t0 + TC],
                            a_u[:, par, blk, t0:t0 + TC], a_y[:, :]), sem="sv")
                        v_u_prev = em.c["sv"]
                        if blk == 1:
                            v_bc_used_par[pa] = em.c["sv"]
                        if stage == "A" and c == 0 and s == 0 and blk == 0:
                            v_dump = em.c["sv"]
                            em.wait("gpsimd", "sv", v_dump)
                            em.dma("gpsimd", "qg", lambda e: (dbg_dt[:, :, :], a_dt[:, :, :]))
                            em.dma("gpsimd", "qg", lambda e: (dbg_xln[:, :, :], a_xln[:, :, :]))
                            em.dma("gpsimd", "qg", lambda e: (dbg_rs[:, :], a_rs[:, :]))
                            em.dma("gpsimd", "qg", lambda e: (dbg_mu[:, :], a_mu[:, :]))
                            em.dma("gpsimd", "qg", lambda e: (dbg_B[:, :], a_Bf[:, :]))
                            em.dma("gpsimd", "qg", lambda e: (dbg_ab[:, :], a_ab[:, :]))
                            em.dma("gpsimd", "qg", lambda e: (dbg_bb[:, :], a_bb[:, :]))
                            em.dma("gpsimd", "qg", lambda e: (dbg_hh[:, :], a_hh[:, :]))
                            em.dma("gpsimd", "qg", lambda e: (dbg_A[:, :], s_A[:, :]))
                            em.dma("gpsimd", "qg", lambda e: (dbg_hs[:, :], a_hs[:, :]))
                            em.dma("gpsimd", "qg", lambda e: (dbg_rA[:, :], s_rA[:, :]))
                            em.dma("gpsimd", "qg", lambda e: (dbg_Cb[:, :], a_Cb[:, 0]))
                            em.dma("gpsimd", "qg", lambda e: (dbg_C[:, :], a_Cf[:, :]))
                            em.dma("gpsimd", "qg", lambda e: (dbg_y[:, :], a_y[:, :]))
                            em.dma("gpsimd", "qg", lambda e: (dbg_hc[:, :], a_b0[:, :]))
                            v_dumped = em.c["qg"]
                            for en in ("vector", "scalar", "tensor"):
                                em.wait(en, "qg", v_dumped)
                    v_bc_used = em.c["sv"]
                v_scan_done[par] = em.c["sv"]
                em.wait("sync", "sv", v_scan_done[par])
                v_ustore[c + 2] = em.dma("sync", "qs", lambda e, c=c, par=par: (
                    su_r[:, :, c * TA:(c + 1) * TA], a_u[:, par]))

                pv_d1, pv_dt, pv_bcfix = v_d1, v_dt, v_bcfix
                pv_dtraw, pv_strow, pv_bcrows, pv_xln = v_dtraw, v_strow, v_bcrows, v_xln

            v_all_u = em.c["qs"]
            if stage == "A":
                em.wait("sync", "qs", v_all_u)
                em.wait("gpsimd", "qs", v_all_u)
                return

            em.wait("gpsimd", "qs", v_all_u)
            em.op("gpsimd", lambda e: e.collective_compute(
                "AllGather", OP.bypass,
                replica_groups=[[0, 1, 2, 3], [4, 5, 6, 7]],
                ins=[sendu[:, :]], outs=[recvu[:, :]]), sem="cc")
            v_cc = em.c["cc"]
            for en in ENGINES:
                em.wait(en, "cc", v_cc)

            v_cu = em.dma("sync", "qs", lambda e: (
                c_u[:, :, :], rv_r[:, :, bass.ts(em.q(), TQ)]))
            if stage == "B":
                em.wait("sync", "qs", v_cu)
                v_outB = em.dma("sync", "qs", lambda e: (out_r[:, :, :], c_u[:, :, :]))
                em.wait("sync", "qs", v_outB)
                em.wait("gpsimd", "qs", v_outB)
                return


            def _cut(n, waits):
                if stage == "C1" and cut == n:
                    for en, sem, val in waits:
                        em.wait(en, sem, val)
                    vo = em.dma("sync", "qs", lambda e: (out_r[:, :, :], c_u[:, :, :]))
                    em.wait("sync", "qs", vo)
                    em.wait("gpsimd", "qs", vo)
                    return True
                return False

            if _cut(0, [("sync", "qs", v_cu)]):
                return

            ub_free = {0: 0, 1: 0}
            csq_free = {0: 0, 1: 0}
            for blk in range(NBLK):
                em.wait("gpsimd", "st", ub_free[blk % 2])
                v_ub = em.dma("gpsimd", "qg", lambda e, blk=blk: (
                    c_ub[:, blk % 2], rv_r[:, blk, bass.ts(em.q(), TQ)]))
                em.wait("scalar", "qg", v_ub)
                em.wait("scalar", "st", csq_free[blk % 2])
                em.op("scalar", lambda e, blk=blk: e.activation(
                    out=c_sq[:, blk % 2], in_=c_ub[:, blk % 2], func=AF.Square),
                    sem="sk")
                v_sq2 = em.c["sk"]
                em.wait("tensor", "qg", v_ub)
                for tt in range(4):
                    em.op("tensor", lambda e, blk=blk, tt=tt: e.matmul(
                        ps[tt][0:1, :], s_ones[:, :],
                        c_ub[:, blk % 2, tt * 512:(tt + 1) * 512],
                        start=(blk == 0), stop=(blk == NBLK - 1)),
                        sem="st" if tt == 3 else None)
                ub_free[blk % 2] = em.c["st"]
                em.wait("tensor", "sk", v_sq2)
                for tt in range(4):
                    em.op("tensor", lambda e, blk=blk, tt=tt: e.matmul(
                        ps[4 + tt][0:1, :], s_ones[:, :],
                        c_sq[:, blk % 2, tt * 512:(tt + 1) * 512],
                        start=(blk == 0), stop=(blk == NBLK - 1)),
                        sem="st" if tt == 3 else None)
                csq_free[blk % 2] = em.c["st"]
            v_stat2 = em.c["st"]
            em.wait("vector", "st", v_stat2)
            for tt in range(4):
                sl = slice(tt * 512, (tt + 1) * 512)
                em.op("vector", lambda e, tt=tt, sl=sl: e.tensor_scalar_mul(
                    crow(0, sl), ps[tt][0:1, :], 1.0 / D))
                em.op("vector", lambda e, tt=tt, sl=sl: e.tensor_scalar_mul(
                    crow(1, sl), ps[4 + tt][0:1, :], 1.0 / D),
                    sem="sv" if tt == 3 else None)
            v_ev2 = em.c["sv"]
            if _cut(1, [("sync", "sv", v_ev2), ("sync", "qs", v_cu)]):
                return
            em.wait("sync", "sv", v_ev2)
            em.dma("sync", "qs", lambda e: (statc[0:1, :], crow(0)))
            v_str2 = em.dma("sync", "qs", lambda e: (statc[1:2, :], crow(1)))
            em.wait("gpsimd", "qs", v_str2)
            em.dma("gpsimd", "qg", lambda e: (
                c_mu[:, :], statc[0, :].unsqueeze(0).partition_broadcast(128)))
            v_bc2 = em.dma("gpsimd", "qg", lambda e: (
                c_v2[:, :], statc[1, :].unsqueeze(0).partition_broadcast(128)))
            em.wait("vector", "qg", v_bc2)
            em.op("vector", lambda e: e.tensor_tensor(
                c_t32[:, :], c_mu[:, :], c_mu[:, :], OP.mult))
            em.op("vector", lambda e: e.tensor_sub(
                c_v2[:, :], c_v2[:, :], c_t32[:, :]), sem="sv")
            v_var2 = em.c["sv"]
            em.wait("scalar", "sv", v_var2)
            em.op("scalar", lambda e: e.activation(
                out=c_t32[:, :], in_=c_v2[:, :], func=AF.Ln, bias=s_ct[:, 3:4]))
            em.op("scalar", lambda e: e.activation(
                out=c_t32[:, :], in_=c_t32[:, :], func=AF.Exp, scale=s_ct[:, 0:1]),
                sem="sk")
            v_r1 = em.c["sk"]
            em.wait("vector", "sk", v_r1)
            em.op("vector", lambda e: e.tensor_tensor(
                c_v2[:, :], c_v2[:, :], c_t32[:, :], OP.mult))
            em.op("vector", lambda e: e.tensor_tensor(
                c_v2[:, :], c_v2[:, :], c_t32[:, :], OP.mult), sem="sv")
            v_v2 = em.c["sv"]
            em.wait("scalar", "sv", v_v2)
            em.op("scalar", lambda e: e.activation(
                out=c_v2[:, :], in_=c_v2[:, :], func=AF.Ln, bias=s_ct[:, 3:4]))
            em.op("scalar", lambda e: e.activation(
                out=c_v2[:, :], in_=c_v2[:, :], func=AF.Exp, scale=s_ct[:, 0:1]),
                sem="sk")
            v_r2 = em.c["sk"]
            em.wait("vector", "sk", v_r2)
            em.op("vector", lambda e: e.tensor_tensor(
                c_al[:, :], c_t32[:, :], c_v2[:, :], OP.mult), sem="sv")
            v_alpha = em.c["sv"]
            if _cut(2, [("sync", "sv", v_alpha), ("sync", "qs", v_cu)]):
                return

            em.wait("vector", "sv", v_alpha)
            em.wait("vector", "qs", v_cu)
            for blk in range(NBLK):
                em.op("vector", lambda e, blk=blk: e.tensor_sub(
                    c_t32[:, :], c_u[:, blk], c_mu[:, :]))
                em.op("vector", lambda e: e.tensor_tensor(
                    c_t32[:, :], c_t32[:, :], c_al[:, :], OP.mult))
                em.op("vector", lambda e, blk=blk: e.tensor_scalar(
                    c_nb[:, blk], c_t32[:, :], s_gm[:, blk:blk + 1],
                    s_bt[:, blk:blk + 1], OP.mult, OP.add),
                    sem="sv" if blk == NBLK - 1 else None)
            v_normed = em.c["sv"]
            if _cut(3, [("sync", "sv", v_normed)]):
                return

            em.wait("tensor", "sv", v_normed)
            for blk in range(NBLK):
                for tt in range(4):
                    em.op("tensor", lambda e, blk=blk, tt=tt: e.matmul(
                        ps[tt][0:E, :], s_rt[:, blk * E:(blk + 1) * E],
                        c_nb[:, blk, tt * 512:(tt + 1) * 512],
                        start=(blk == 0), stop=(blk == NBLK - 1)),
                        sem="st" if (blk == NBLK - 1 and tt == 3) else None)
            v_logit = em.c["st"]
            em.wait("vector", "st", v_logit)
            for tt in range(4):
                em.op("vector", lambda e, tt=tt: e.tensor_copy(
                    c_lg[:, tt * 512:(tt + 1) * 512], ps[tt][0:E, :]),
                    sem="sv" if tt == 3 else None)
            v_lgd = em.c["sv"]
            if _cut(4, [("sync", "sv", v_lgd)]):
                return
            em.wait("tensor", "sv", v_lgd)
            for ti in range(16):
                em.op("tensor", lambda e, ti=ti: e.transpose(
                    ps[4][:, ti * E:(ti + 1) * E],
                    c_lg[:, ti * 128:(ti + 1) * 128], s_eye[0:E, 0:E]),
                    sem="st" if ti == 15 else None)
            v_lgT = em.c["st"]
            em.wait("vector", "st", v_lgT)
            em.op("vector", lambda e: e.tensor_copy(c_lgT[:, :], ps[4][:, 0:128]))
            for ti in range(16):
                em.op("vector", lambda e, ti=ti: e.tensor_reduce(
                    c_sc[:, ti:ti + 1], c_lgT[:, ti * E:(ti + 1) * E], AX.X, OP.max))
            em.op("vector", lambda e: e.tensor_scalar_mul(
                c_sc[:, 0:16], c_sc[:, 0:16], -1.0), sem="sv")
            v_nm = em.c["sv"]
            em.wait("vector", "sv", v_nm)
            em.wait("scalar", "sv", v_nm)
            for ti in range(16):
                em.op("scalar", lambda e, ti=ti: e.activation(
                    out=c_ket[:, ti * E:(ti + 1) * E], in_=c_lgT[:, ti * E:(ti + 1) * E],
                    func=AF.Exp, bias=c_sc[:, ti:ti + 1]),
                    sem="sk" if ti == 15 else None)
            v_ket = em.c["sk"]
            em.wait("vector", "sk", v_ket)
            for ti in range(16):
                em.op("vector", lambda e, ti=ti: e.tensor_reduce(
                    c_sc[:, 16 + ti:17 + ti], c_ket[:, ti * E:(ti + 1) * E], AX.X, OP.add))
            for ti in range(16):
                em.op("vector", lambda e, ti=ti: e.tensor_reduce(
                    c_sc[:, 32 + ti:33 + ti], c_ket[:, ti * E:(ti + 1) * E], AX.X, OP.max),
                    sem="sv" if ti == 15 else None)
            v_zs = em.c["sv"]
            em.wait("vector", "sv", v_zs)
            for ti in range(16):
                em.op("vector", lambda e, ti=ti: e.tensor_scalar(
                    c_m1[:, ti * E:(ti + 1) * E], c_ket[:, ti * E:(ti + 1) * E],
                    c_sc[:, 32 + ti:33 + ti], None, OP.is_ge),
                    sem="sv" if ti == 15 else None)
            em.wait("vector", "sv", em.c["sv"])
            for ti in range(16):
                em.op("vector", lambda e, ti=ti: e.scalar_tensor_tensor(
                    c_md[:, ti * E:(ti + 1) * E], c_m1[:, ti * E:(ti + 1) * E],
                    -1e30, c_ket[:, ti * E:(ti + 1) * E], OP.mult, OP.add),
                    sem="sv" if ti == 15 else None)
            em.wait("vector", "sv", em.c["sv"])
            for ti in range(16):
                em.op("vector", lambda e, ti=ti: e.tensor_reduce(
                    c_sc[:, 48 + ti:49 + ti], c_md[:, ti * E:(ti + 1) * E], AX.X, OP.max),
                    sem="sv" if ti == 15 else None)
            v_v2s = em.c["sv"]
            em.wait("vector", "sv", v_v2s)
            em.op("vector", lambda e: e.tensor_sub(
                c_sc[:, 64:80], c_sc[:, 32:48], c_sc[:, 48:64]), sem="sv")
            em.wait("vector", "sv", em.c["sv"])
            em.op("vector", lambda e: e.reciprocal(c_sc[:, 96:112], c_sc[:, 16:32]),
                  sem="sv")
            em.wait("vector", "sv", em.c["sv"])
            em.op("vector", lambda e: e.tensor_tensor(
                c_sc[:, 64:80], c_sc[:, 64:80], c_sc[:, 96:112], OP.mult), sem="sv")
            em.wait("vector", "sv", em.c["sv"])
            v_t = em.c["sv"]
            em.wait("scalar", "sv", v_t)
            em.op("scalar", lambda e: e.activation(
                out=c_sc[:, 80:96], in_=c_sc[:, 64:80], func=AF.Exp, scale=s_ct[:, 2:3]),
                sem="sk")
            v_en = em.c["sk"]
            em.wait("vector", "sk", v_en)
            em.op("vector", lambda e: e.tensor_scalar_add(
                c_sc[:, 96:112], c_sc[:, 80:96], 1.0), sem="sv")
            em.wait("vector", "sv", em.c["sv"])
            em.op("vector", lambda e: e.reciprocal(c_sc[:, 64:80], c_sc[:, 96:112]),
                  sem="sv")
            em.wait("vector", "sv", em.c["sv"])
            em.op("vector", lambda e: e.tensor_scalar(
                c_sc[:, 80:96], c_sc[:, 64:80], -1.0, 1.0, OP.mult, OP.add), sem="sv")
            em.wait("vector", "sv", em.c["sv"])
            v_g = em.c["sv"]
            em.wait("vector", "sv", v_g)
            for ti in range(16):
                em.op("vector", lambda e, ti=ti: e.tensor_scalar(
                    c_cmb[:, ti * E:(ti + 1) * E], c_m1[:, ti * E:(ti + 1) * E],
                    c_sc[:, 64 + ti:65 + ti], None, OP.mult),
                    sem="sv" if ti == 15 else None)
            em.wait("vector", "sv", em.c["sv"])
            for ti in range(16):
                em.op("vector", lambda e, ti=ti: e.tensor_scalar(
                    c_m1[:, ti * E:(ti + 1) * E], c_md[:, ti * E:(ti + 1) * E],
                    c_sc[:, 48 + ti:49 + ti], None, OP.is_ge),
                    sem="sv" if ti == 15 else None)
            em.wait("vector", "sv", em.c["sv"])
            for ti in range(16):
                em.op("vector", lambda e, ti=ti: e.scalar_tensor_tensor(
                    c_cmb[:, ti * E:(ti + 1) * E], c_m1[:, ti * E:(ti + 1) * E],
                    c_sc[:, 80 + ti:81 + ti], c_cmb[:, ti * E:(ti + 1) * E],
                    OP.mult, OP.add), sem="sv" if ti == 15 else None)
            v_cmb = em.c["sv"]
            if _cut(5, [("sync", "sv", v_cmb)]):
                return
            em.wait("tensor", "sv", v_cmb)
            v_cpg = 0
            for g in range(4):
                em.wait("tensor", "sv", v_cpg)
                for i in range(4):
                    ti = g * 4 + i
                    em.op("tensor", lambda e, ti=ti, i=i: e.transpose(
                        ps[5][0:E, i * 128:(i + 1) * 128],
                        c_cmb[:, ti * E:(ti + 1) * E], s_eye[:, :]),
                        sem="st" if i == 3 else None)
                v_tr = em.c["st"]
                em.wait("vector", "st", v_tr)
                em.op("vector", lambda e, g=g: e.tensor_copy(
                    c_lg[:, g * 512:(g + 1) * 512], ps[5][0:E, 0:512]),
                    sem="sv")
                v_cpg = em.c["sv"]
            v_cT = em.c["sv"]
            em.wait("sync", "sv", v_cT)
            v_combh = em.dma("sync", "qs", lambda e: (combh[:, :], c_lg[:, :]))
            if stage == "C2":
                em.wait("gpsimd", "qs", v_combh)
                em.dma("gpsimd", "qg", lambda e: (out_r[:, 0, 0:128], c_lgT[:, :]))
                em.dma("gpsimd", "qg", lambda e: (out_r[:, 1, 0:128], c_ket[:, :]))
                em.dma("gpsimd", "qg", lambda e: (out_r[:, 2, 0:112], c_sc[:, :]))
                em.dma("gpsimd", "qg", lambda e: (out_r[:, 3, 0:128], c_m1[:, :]))
                em.dma("gpsimd", "qg", lambda e: (out_r[:, 4, 0:128], c_md[:, :]))
                em.dma("gpsimd", "qg", lambda e: (out_r[:, 5, 0:128], c_cmb[:, :]))
                vd = em.dma("gpsimd", "qg", lambda e: (
                    out_r[:, 6, 0:TQ],
                    combh[0, :].unsqueeze(0).partition_broadcast(128)))
                em.wait("gpsimd", "qg", vd)
                em.wait("sync", "qs", em.c["qs"])
                return
            if _cut(6, [("sync", "qs", v_combh)]):
                return

            if stage == "C1":
                em.wait("gpsimd", "sv", v_normed)
                em.wait("gpsimd", "qs", v_combh)
                v_o1 = em.dma("gpsimd", "qg", lambda e: (out_r[:, :, :], c_nb[:, :, :]))
                em.wait("gpsimd", "qg", v_o1)
                em.wait("sync", "qs", em.c["qs"])
                return

            gu_free_sk = {0: 0, 1: 0}
            gu_free_sv = {0: v_lgd, 1: v_cT}
            csg_free = {0: 0, 1: 0}
            dn_free = {0: 0, 1: 0}
            v_wg_free = {0: 0, 1: 0}
            v_wd_free = {0: 0, 1: 0}
            v_dn_mm_all = 0
            v_cb_used = 0
            v_out = 0

            elist = list(range(E + 1)) if experts is None else list(experts)
            for e_i in elist:
                routed = e_i < E
                NM = 24 if routed else 8
                NF = 24 if routed else 8
                if routed:
                    em.wait("gpsimd", "qs", v_combh)
                    em.wait("gpsimd", "sv", v_cb_used)
                    v_cb = em.dma("gpsimd", "qg", lambda e, e_i=e_i: (
                        c_cb[:, :],
                        combh[e_i, :].unsqueeze(0).partition_broadcast(128)))
                else:
                    v_cb = 0
                for th in range(2):
                    tof = th * 1024
                    v_gmm = {}
                    v_umm = {}
                    v_sil = {}
                    v_h = {}
                    for mg in range(NM):
                        pw = mg % 2
                        em.wait("gpsimd", "st", v_wg_free[pw])
                        if routed:
                            em.dma("gpsimd", "qg", lambda e, e_i=e_i, mg=mg, pw=pw: (
                                c_wg[:, pw], guN[e_i, mg]))
                            v_w = em.dma("gpsimd", "qg", lambda e, e_i=e_i, mg=mg, pw=pw: (
                                c_wu[:, pw], guN[e_i, 24 + mg]))
                        else:
                            em.dma("gpsimd", "qg", lambda e, mg=mg, pw=pw: (
                                c_wg[:, pw], sgN[mg]))
                            v_w = em.dma("gpsimd", "qg", lambda e, mg=mg, pw=pw: (
                                c_wu[:, pw], sgN[8 + mg]))
                        em.wait("tensor", "qg", v_w)
                        em.wait("tensor", "sk", gu_free_sk[pw])
                        em.wait("tensor", "sv", gu_free_sv[pw])
                        for k in range(NBLK):
                            for tt in range(2):
                                em.op("tensor", lambda e, k=k, tt=tt, pw=pw, tof=tof:
                                      e.matmul(
                                          ps[4 * pw + tt][:, :],
                                          c_wg[:, pw, k * 128:(k + 1) * 128],
                                          c_nb[:, k, tof + tt * 512:tof + (tt + 1) * 512],
                                          start=(k == 0), stop=(k == NBLK - 1)),
                                      sem="st" if (k == NBLK - 1 and tt == 1) else None)
                        v_gmm[mg] = em.c["st"]
                        for k in range(NBLK):
                            for tt in range(2):
                                em.op("tensor", lambda e, k=k, tt=tt, pw=pw, tof=tof:
                                      e.matmul(
                                          ps[4 * pw + 2 + tt][:, :],
                                          c_wu[:, pw, k * 128:(k + 1) * 128],
                                          c_nb[:, k, tof + tt * 512:tof + (tt + 1) * 512],
                                          start=(k == 0), stop=(k == NBLK - 1)),
                                      sem="st" if (k == NBLK - 1 and tt == 1) else None)
                        v_umm[mg] = em.c["st"]
                        v_wg_free[pw] = v_umm[mg]
                        em.wait("scalar", "st", v_gmm[mg])
                        em.wait("scalar", "sv", csg_free[pw])
                        for tt in range(2):
                            em.op("scalar", lambda e, tt=tt, pw=pw: e.activation(
                                out=c_sg[:, pw, tt * 512:(tt + 1) * 512],
                                in_=ps[4 * pw + tt][:, :], func=AF.Silu),
                                sem="sk" if tt == 1 else None)
                        v_sil[mg] = em.c["sk"]
                        gu_free_sk[pw] = v_sil[mg]
                        em.wait("vector", "st", v_umm[mg])
                        em.wait("vector", "sk", v_sil[mg])
                        em.wait("vector", "st", v_dn_mm_all)
                        if routed and not no_comb:
                            em.wait("vector", "qg", v_cb)
                            for tt in range(2):
                                em.op("vector", lambda e, tt=tt, pw=pw, tof=tof:
                                      e.tensor_tensor(
                                          c_uc[:, tt * 512:(tt + 1) * 512],
                                          ps[4 * pw + 2 + tt][:, :],
                                          c_cb[:, tof + tt * 512:tof + (tt + 1) * 512],
                                          OP.mult))
                                em.op("vector", lambda e, tt=tt, mg=mg, pw=pw:
                                      e.tensor_tensor(
                                          c_hb[:, mg, tt * 512:(tt + 1) * 512],
                                          c_sg[:, pw, tt * 512:(tt + 1) * 512],
                                          c_uc[:, tt * 512:(tt + 1) * 512], OP.mult),
                                      sem="sv" if tt == 1 else None)
                        else:
                            for tt in range(2):
                                em.op("vector", lambda e, tt=tt, mg=mg, pw=pw:
                                      e.tensor_tensor(
                                          c_hb[:, mg, tt * 512:(tt + 1) * 512],
                                          c_sg[:, pw, tt * 512:(tt + 1) * 512],
                                          ps[4 * pw + 2 + tt][:, :], OP.mult),
                                      sem="sv" if tt == 1 else None)
                        v_h[mg] = em.c["sv"]
                        gu_free_sv[pw] = v_h[mg]
                        csg_free[pw] = v_h[mg]
                    v_hall = v_h[NM - 1]
                    v_cb_used = v_hall
                    for md in range(0 if no_down else NBLK):
                        pw2 = md % 2
                        em.wait("gpsimd", "st", v_wd_free[pw2])
                        if routed:
                            v_wd = em.dma("gpsimd", "qg", lambda e, e_i=e_i, md=md, pw2=pw2:
                                          (c_wd[:, pw2, 0:NF * 128], dnN[e_i, md]))
                        else:
                            v_wd = em.dma("gpsimd", "qg", lambda e, md=md, pw2=pw2:
                                          (c_wd[:, pw2, 0:NF * 128], sdN[md]))
                        em.wait("tensor", "qg", v_wd)
                        em.wait("tensor", "sv", v_hall)
                        em.wait("tensor", "sv", dn_free[pw2])
                        em.wait("tensor", "sk", gu_free_sk[0])
                        em.wait("tensor", "sv", gu_free_sv[0])
                        for f in range(NF):
                            for tt in range(2):
                                em.op("tensor", lambda e, f=f, tt=tt, pw2=pw2: e.matmul(
                                    ps[2 * pw2 + tt][:, :],
                                    c_wd[:, pw2, f * 128:(f + 1) * 128],
                                    c_hb[:, f, tt * 512:(tt + 1) * 512],
                                    start=(f == 0), stop=(f == NF - 1)),
                                    sem="st" if (f == NF - 1 and tt == 1) else None)
                        v_dmm = em.c["st"]
                        v_wd_free[pw2] = v_dmm
                        v_dn_mm_all = v_dmm
                        em.wait("vector", "st", v_dmm)
                        for tt in range(2):
                            em.op("vector", lambda e, md=md, tt=tt, pw2=pw2, tof=tof:
                                  e.scalar_tensor_tensor(
                                      c_u[:, md, tof + tt * 512:tof + (tt + 1) * 512],
                                      ps[2 * pw2 + tt][:, :], 1.0,
                                      c_u[:, md, tof + tt * 512:tof + (tt + 1) * 512],
                                      OP.mult, OP.add),
                                  sem="sv" if tt == 1 else None)
                        dn_free[pw2] = em.c["sv"]
                    gu_free_sv[0] = max(gu_free_sv[0], dn_free[0], dn_free[1])

            v_fin = max(dn_free[0], dn_free[1])
            em.wait("sync", "sv", v_fin)
            v_out = em.dma("sync", "qs", lambda e: (out_r[:, :, :], c_u[:, :, :]))
            em.wait("sync", "qs", v_out)
            em.wait("gpsimd", "qs", v_out)

        @block.tensor
        def _(eng):
            program(Em("tensor", eng, sems))

        @block.vector
        def _(eng):
            program(Em("vector", eng, sems))

        @block.scalar
        def _(eng):
            program(Em("scalar", eng, sems))

        @block.gpsimd
        def _(eng):
            program(Em("gpsimd", eng, sems))

        @block.sync
        def _(eng):
            program(Em("sync", eng, sems))

    return nc


def _prep_host(inputs):
    f32 = np.float32
    x = np.asarray(inputs["L0_attn_x"], dtype=f32)
    W_B = np.asarray(inputs["L0_attn_W_B"], dtype=f32)
    W_C = np.asarray(inputs["L0_attn_W_C"], dtype=f32)
    W_dt1 = np.asarray(inputs["L0_attn_W_dt1"], dtype=f32)
    W_dt2 = np.asarray(inputs["L0_attn_W_dt2"], dtype=f32)
    b_dt2 = np.asarray(inputs["L0_attn_b_dt2"], dtype=f32)
    log_A = np.asarray(inputs["L0_attn_log_A"], dtype=f32)
    D_skip = np.asarray(inputs["L0_attn_D_skip"], dtype=f32)
    gam = np.asarray(inputs["L0_ffn_ln_gamma"], dtype=f32)
    bet = np.asarray(inputs["L0_ffn_ln_beta"], dtype=f32)
    rw = np.asarray(inputs["L0_ffn_router_w"], dtype=f32)
    guw = np.asarray(inputs["L0_ffn_gate_up_w"], dtype=f32)
    dnw = np.asarray(inputs["L0_ffn_down_w"], dtype=f32)
    sgw = np.asarray(inputs["L0_ffn_shared_gate_up_w"], dtype=f32)
    sdw = np.asarray(inputs["L0_ffn_shared_down_w"], dtype=f32)

    ac = np.ascontiguousarray
    xT = [ac(x[b].T) for b in range(2)]

    def blkcols(w):
        m = w.shape[1]
        return ac(w.reshape(NBLK, 128, m).transpose(1, 0, 2).reshape(128, NBLK * m))

    wbN = blkcols(ac(W_B.T))
    wcN = blkcols(ac(W_C.T))
    w1N = blkcols(ac(W_dt1.T))
    rtN = blkcols(ac(rw.T))
    gmN = ac(gam.reshape(NBLK, 128).T)
    btN = ac(bet.reshape(NBLK, 128).T)
    eyN = np.eye(128, dtype=f32)
    ctN = np.tile(np.array([[-0.5, 1.0, -1.0, LN_EPS]], dtype=f32), (128, 1))

    guT = guw.transpose(0, 2, 1)
    guN = ac(guT.reshape(E, NBLK, 128, 48, 128)
             .transpose(0, 3, 2, 1, 4).reshape(E, 48, 128, NBLK * 128))
    dnT = dnw.transpose(0, 2, 1)
    dnN = ac(dnT.reshape(E, 24, 128, NBLK, 128)
             .transpose(0, 3, 2, 1, 4).reshape(E, NBLK, 128, 24 * 128))
    sgT = ac(sgw.T)
    sgN = ac(sgT.reshape(NBLK, 128, 16, 128)
             .transpose(2, 1, 0, 3).reshape(16, 128, NBLK * 128))
    sdT = ac(sdw.T)
    sdN = ac(sdT.reshape(NBLK, 128, NBLK, 128)
             .transpose(2, 1, 0, 3).reshape(NBLK, 128, NBLK * 128))

    in_maps = []
    for k in range(8):
        b, qq = divmod(k, 4)
        sl = slice(qq * CH, (qq + 1) * CH)
        in_maps.append({
            "xf": xT[b],
            "xo": ac(xT[b][sl]),
            "wbN": wbN, "wcN": wcN, "w1N": w1N,
            "w2N": ac(W_dt2.T[:, sl]),
            "bdN": ac(b_dt2[sl].reshape(2, 128).T),
            "laN": ac(log_A[sl].reshape(2, 128, NST).transpose(1, 0, 2)
                      .reshape(128, 2 * NST)),
            "dsN": ac(D_skip[sl].reshape(2, 128).T),
            "gmN": gmN, "btN": btN, "rtN": rtN, "eyN": eyN, "ctN": ctN,
            "guN": guN, "dnN": dnN, "sgN": sgN, "sdN": sdN,
        })
    return in_maps


_NC_CACHE = {}


def kernel(**inputs):
    in_maps = _prep_host(inputs)
    if "full" not in _NC_CACHE:
        _NC_CACHE["full"] = build("full")
    nc = _NC_CACHE["full"]
    res = run_bass_kernel_spmd(nc, in_maps, core_ids=list(range(8)))
    full = np.empty((2, D, TB), np.float32)
    for k in range(8):
        b, qq = divmod(k, 4)
        full[b][:, qq * TQ:(qq + 1) * TQ] = res.results[k]["out"]
    return np.ascontiguousarray(full.transpose(0, 2, 1))


# version 04
# speedup vs baseline: 1.0057x; 1.0057x over previous
import numpy as np
from contextlib import ExitStack

import concourse.bass as bass
import concourse.mybir as mybir
from concourse.bass_utils import run_bass_kernel_spmd

F32 = mybir.dt.float32
BF16 = mybir.dt.bfloat16
AF = mybir.ActivationFunctionType
OP = mybir.AluOpType
AX = mybir.AxisListType

D = 1024
NST = 16
RDT = 64
E = 8
TB = 8192
TQ = 2048
CH = 256
NBLK = 8
TA = 1024
NCHUNK = TB // TA
TC = 256
NSUB = TA // TC
LN_EPS = 1e-5

SEMS = ("qg", "qs", "sv", "sk", "st", "cc", "sg")
ENGINES = ("tensor", "vector", "scalar", "gpsimd", "sync")


class Em:

    def __init__(self, name, eng, sems):
        self.name = name
        self.eng = eng
        self.sems = sems
        self.c = {k: 0 for k in sems}
        self._q = None

    def inc(self, s, n):
        self.c[s] += n
        return self.c[s]

    def dma(self, qname, sem, fn):
        v = self.inc(sem, 16)
        if self.name == qname:
            out, in_ = fn(self.eng)
            self.eng.dma_start(out=out, in_=in_).then_inc(self.sems[sem], 16)
        return v

    def op(self, ename, fn, sem=None):
        v = self.inc(sem, 1) if sem else None
        if self.name == ename:
            ins = fn(self.eng)
            if sem:
                ins.then_inc(self.sems[sem], 1)
        return v

    def wait(self, ename, sem, val):
        if self.name == ename and val and val > 0:
            self.eng.wait_ge(self.sems[sem], val)

    def q(self):
        if self._q is None:
            pid = self.eng.partition_id()
            self._q = pid % 4
        return self._q


def build(stage="full", cut=99, experts=None, no_comb=False, no_down=False):
    nc = bass.Bass("TRN2", target_bir_lowering=False, debug=False)

    xf = nc.dram_tensor("xf", [D, TB], F32, kind="ExternalInput").ap()
    xo = nc.dram_tensor("xo", [CH, TB], F32, kind="ExternalInput").ap()
    wbN = nc.dram_tensor("wbN", [128, NBLK * NST], F32, kind="ExternalInput").ap()
    wcN = nc.dram_tensor("wcN", [128, NBLK * NST], F32, kind="ExternalInput").ap()
    w1N = nc.dram_tensor("w1N", [128, NBLK * RDT], F32, kind="ExternalInput").ap()
    w2N = nc.dram_tensor("w2N", [RDT, CH], F32, kind="ExternalInput").ap()
    bdN = nc.dram_tensor("bdN", [128, 2], F32, kind="ExternalInput").ap()
    laN = nc.dram_tensor("laN", [128, 2 * NST], F32, kind="ExternalInput").ap()
    dsN = nc.dram_tensor("dsN", [128, 2], F32, kind="ExternalInput").ap()
    gmN = nc.dram_tensor("gmN", [128, NBLK], F32, kind="ExternalInput").ap()
    btN = nc.dram_tensor("btN", [128, NBLK], F32, kind="ExternalInput").ap()
    rtN = nc.dram_tensor("rtN", [128, NBLK * E], F32, kind="ExternalInput").ap()
    eyN = nc.dram_tensor("eyN", [128, 128], F32, kind="ExternalInput").ap()
    ctN = nc.dram_tensor("ctN", [128, 4], F32, kind="ExternalInput").ap()
    guN = nc.dram_tensor("guN", [E, 48, 128, NBLK * 128], F32, kind="ExternalInput").ap()
    dnN = nc.dram_tensor("dnN", [E, 8, 128, 24 * 128], F32, kind="ExternalInput").ap()
    sgN = nc.dram_tensor("sgN", [16, 128, NBLK * 128], F32, kind="ExternalInput").ap()
    sdN = nc.dram_tensor("sdN", [8, 128, NBLK * 128], F32, kind="ExternalInput").ap()

    stath = nc.dram_tensor("stath", [2, TB], F32).ap()
    bch = nc.dram_tensor("bch", [2, TB // TC, NST * TC], F32).ap()
    combh = nc.dram_tensor("combh", [E, TQ], F32).ap()
    statc = nc.dram_tensor("statc", [2, TQ], F32).ap()
    if stage == "A":
        sendu = nc.dram_tensor("udev", [CH, TB], F32, kind="ExternalOutput").ap()
        dbg_dt = nc.dram_tensor("dbg_dt", [128, 2, TA], F32, kind="ExternalOutput").ap()
        dbg_xln = nc.dram_tensor("dbg_xln", [128, 2, TA], F32, kind="ExternalOutput").ap()
        dbg_rs = nc.dram_tensor("dbg_rs", [128, TA], F32, kind="ExternalOutput").ap()
        dbg_mu = nc.dram_tensor("dbg_mu", [128, TA], F32, kind="ExternalOutput").ap()
        dbg_B = nc.dram_tensor("dbg_B", [NST, TA], F32, kind="ExternalOutput").ap()
        dbg_ab = nc.dram_tensor("dbg_ab", [128, NST * TC], F32, kind="ExternalOutput").ap()
        dbg_bb = nc.dram_tensor("dbg_bb", [128, NST * TC], F32, kind="ExternalOutput").ap()
        dbg_hh = nc.dram_tensor("dbg_hh", [128, NST * TC], F32, kind="ExternalOutput").ap()
        dbg_A = nc.dram_tensor("dbg_A", [128, 2 * NST], F32, kind="ExternalOutput").ap()
        dbg_hs = nc.dram_tensor("dbg_hs", [128, 2 * NST], F32, kind="ExternalOutput").ap()
        dbg_rA = nc.dram_tensor("dbg_rA", [128, 2 * NST], F32, kind="ExternalOutput").ap()
        dbg_Cb = nc.dram_tensor("dbg_Cb", [128, NST * TC], F32, kind="ExternalOutput").ap()
        dbg_C = nc.dram_tensor("dbg_C", [NST, TA], F32, kind="ExternalOutput").ap()
        dbg_y = nc.dram_tensor("dbg_y", [128, TC], F32, kind="ExternalOutput").ap()
        dbg_hc = nc.dram_tensor("dbg_hc", [128, NST * TC], F32, kind="ExternalOutput").ap()
        recvu, out = None, None
    else:
        sendu = nc.dram_tensor("sendu", [CH, TB], F32).ap()
        recvu = nc.dram_tensor("recvu", [D, TB], F32).ap()
        out = nc.dram_tensor("out", [D, TQ], F32, kind="ExternalOutput").ap()

    xf_r = xf.rearrange("(b p) t -> p b t", p=128)
    xo_r = xo.rearrange("(b p) t -> p b t", p=128)
    su_r = sendu.rearrange("(b p) t -> p b t", p=128)
    if stage != "A":
        rv_r = recvu.rearrange("(b p) t -> p b t", p=128)
        out_r = out.rearrange("(b p) t -> p b t", p=128)

    al = [16512]

    offs = {}

    def sb(name, shape, dtype):
        nbytes = int(np.prod(shape[1:])) * mybir.dt.size(dtype)
        off = al[0]
        offs[name] = off
        al[0] = off + ((nbytes + 63) // 64) * 64
        return nc.alloc_sbuf_tensor_at(name, list(shape), dtype, offset=off).ap()

    s_ones = sb("s_ones", [128, 1], BF16)
    s_eye = sb("s_eye", [128, 128], F32)
    s_ct = sb("s_ct", [128, 4], F32)
    s_wb = sb("s_wb", [128, NBLK * NST], BF16)
    s_wc = sb("s_wc", [128, NBLK * NST], BF16)
    s_w1 = sb("s_w1", [128, NBLK * RDT], BF16)
    s_w2 = sb("s_w2", [RDT, CH], BF16)
    s_rt = sb("s_rt", [128, NBLK * E], BF16)
    s_gm = sb("s_gm", [128, NBLK], F32)
    s_bt = sb("s_bt", [128, NBLK], F32)
    s_A = sb("s_A", [128, 2 * NST], F32)
    s_rA = sb("s_rA", [128, 2 * NST], F32)
    s_ds = sb("s_ds", [128, 2], F32)
    s_bd = sb("s_bd", [128, 2], F32)
    s_nwB = sb("s_nwB", [NST, 1], F32)
    s_nwC = sb("s_nwC", [NST, 1], F32)
    s_nw1 = sb("s_nw1", [RDT, 1], F32)
    arena = al[0]

    a_xf = sb("a_xf", [128, 2, NBLK, TA], BF16)
    a_xo = sb("a_xo", [128, 2, TA], F32)
    a_sq = sb("a_sq", [128, 2, TA], BF16)
    a_mu = sb("a_mu", [128, TA], F32)
    a_rs = sb("a_rs", [128, TA], F32)
    a_xln = sb("a_xln", [128, 2, TA], F32)
    a_dt = sb("a_dt", [128, 2, TA], F32)
    a_Bf = sb("a_Bf", [NST, TA], F32)
    a_Cf = sb("a_Cf", [NST, TA], F32)
    a_d1 = sb("a_d1", [RDT, TA], BF16)
    a_ab = sb("a_ab", [128, NST * TC], F32)
    a_b0 = sb("a_b0", [128, NST * TC], F32)
    a_bb = sb("a_bb", [128, NST * TC], F32)
    a_hh = sb("a_hh", [128, NST * TC], F32)
    a_Bb = sb("a_Bb", [128, 2, NST * TC], BF16)
    a_Cb = sb("a_Cb", [128, 2, NST * TC], BF16)
    a_hs = sb("a_hs", [128, 2 * NST], F32)
    a_u = sb("a_u", [128, 2, 2, TA], F32)
    a_row = sb("a_row", [1, 2, TA], F32)
    _row_off = offs["a_row"]
    a_y = nc.alloc_sbuf_tensor_at("a_y", [128, TC], F32, offset=_row_off).ap()
    a_sp = nc.alloc_sbuf_tensor_at("a_sp", [128, 512], F32,
                                   offset=_row_off + TC * 4).ap()

    def arow(r, sl=slice(None)):
        return a_row[0:1, r, sl]

    a_end = al[0]
    assert a_end <= 229312, f"phase A sbuf overflow: {a_end}"

    al[0] = arena
    c_u = sb("c_u", [128, NBLK, TQ], F32)
    c_nb = sb("c_nb", [128, NBLK, TQ], BF16)
    c_hb = sb("c_hb", [128, 24, 1024], BF16)
    c_mu = sb("c_mu", [128, TQ], BF16)
    c_al = sb("c_al", [128, TQ], BF16)
    persist = al[0]
    c_ub = sb("c_ub", [128, 2, TQ], BF16)
    c_sq = sb("c_sq", [128, 2, TQ], BF16)
    c_row = nc.alloc_sbuf_tensor_at("c_row", [1, 2, TQ], F32,
                                    offset=offs["c_hb"]).ap()
    c_v2 = nc.alloc_sbuf_tensor_at("c_v2", [128, TQ], F32,
                                   offset=offs["c_hb"] + 2 * TQ * 4).ap()

    def crow(r, sl=slice(None)):
        return c_row[0:1, r, sl]

    c_lg = sb("c_lg", [8, TQ], F32)
    c_t32 = sb("c_t32", [128, TQ], F32)
    c_lgT = sb("c_lgT", [128, 16 * E], F32)
    c_ket = sb("c_ket", [128, 16 * E], F32)
    c_m1 = sb("c_m1", [128, 16 * E], F32)
    c_md = sb("c_md", [128, 16 * E], F32)
    c_sc = sb("c_sc", [128, 7 * 16], F32)
    c_cmb = sb("c_cmb", [128, 16 * E], F32)
    c_cT = None
    pro_end = al[0]
    al[0] = persist
    c_cb = sb("c_cb", [128, TQ], BF16)
    c_wg = sb("c_wg", [128, 2, NBLK * 128], BF16)
    c_wu = sb("c_wu", [128, 2, NBLK * 128], BF16)
    c_wd = sb("c_wd", [128, 2, 24 * 128], BF16)
    c_sg = sb("c_sg", [128, 2, 1024], BF16)
    c_uc = sb("c_uc", [128, 1024], BF16)
    loop_end = al[0]
    assert max(pro_end, loop_end) <= 229312, f"phase C sbuf overflow {pro_end}/{loop_end}"

    ps = [nc.alloc_psum_tensor(f"ps{i}", [128, 512], F32).ap() for i in range(8)]

    with ExitStack() as stk:
        sems = {s: stk.enter_context(nc.semaphore(s)) for s in SEMS}
        block = stk.enter_context(nc.Block())

        def program(em):
            em.dma("gpsimd", "qg", lambda e: (s_wb[:, :], wbN[:, :]))
            em.dma("gpsimd", "qg", lambda e: (s_wc[:, :], wcN[:, :]))
            em.dma("gpsimd", "qg", lambda e: (s_w1[:, :], w1N[:, :]))
            em.dma("gpsimd", "qg", lambda e: (s_w2[:, :], w2N[:, :]))
            v_small = em.dma("gpsimd", "qg", lambda e: (s_rt[:, :], rtN[:, :]))
            em.dma("sync", "qs", lambda e: (s_gm[:, :], gmN[:, :]))
            em.dma("sync", "qs", lambda e: (s_bt[:, :], btN[:, :]))
            em.dma("sync", "qs", lambda e: (s_A[:, :], laN[:, :]))
            em.dma("sync", "qs", lambda e: (s_ds[:, :], dsN[:, :]))
            em.dma("sync", "qs", lambda e: (s_eye[:, :], eyN[:, :]))
            em.dma("sync", "qs", lambda e: (s_ct[:, :], ctN[:, :]))
            v_f32 = em.dma("sync", "qs", lambda e: (s_bd[:, :], bdN[:, :]))
            v_xf = [0] * (NCHUNK + 1)
            v_xo = [0] * (NCHUNK + 1)
            v_xf[0] = em.dma("gpsimd", "qg",
                             lambda e: (a_xf[:, 0], xf_r[:, :, 0:TA]))

            em.op("vector", lambda e: e.memset(s_ones[:, :], 1.0))
            em.wait("scalar", "qs", v_f32)
            em.op("scalar", lambda e: e.activation(out=s_A[:, :], in_=s_A[:, :],
                                                   func=AF.Exp), sem="sk")
            v_A0 = em.c["sk"]
            em.wait("vector", "sk", v_A0)
            em.op("vector", lambda e: e.tensor_scalar_mul(s_A[:, :], s_A[:, :], -1.0),
                  sem="sv")
            em.wait("vector", "sv", em.c["sv"])
            em.op("vector", lambda e: e.tensor_scalar_add(a_hs[:, :], s_A[:, :], 1e-8),
                  sem="sv")
            em.wait("vector", "sv", em.c["sv"])
            em.op("vector", lambda e: e.reciprocal(s_rA[:, :], a_hs[:, :]), sem="sv")
            v_A = em.c["sv"]

            em.wait("tensor", "qg", v_small)
            em.wait("tensor", "sv", v_A)

            def wsum(dst, w, m):
                for k in range(NBLK):
                    em.op("tensor", lambda e, k=k: e.matmul(
                        dst, w[:, k * m:(k + 1) * m], s_ones[:, :],
                        start=(k == 0), stop=(k == NBLK - 1)),
                        sem="st" if k == NBLK - 1 else None)
            wsum(ps[0][0:NST, 0:1], s_wb, NST)
            wsum(ps[1][0:NST, 0:1], s_wc, NST)
            wsum(ps[2][0:RDT, 0:1], s_w1, RDT)
            v_ws = em.c["st"]
            em.wait("vector", "st", v_ws)
            em.op("vector", lambda e: e.tensor_scalar_mul(s_nwB[:, :], ps[0][0:NST, 0:1], -1.0))
            em.op("vector", lambda e: e.tensor_scalar_mul(s_nwC[:, :], ps[1][0:NST, 0:1], -1.0))
            em.op("vector", lambda e: e.tensor_scalar_mul(s_nw1[:, :], ps[2][0:RDT, 0:1], -1.0),
                  sem="sv")
            v_setup = em.c["sv"]
            em.wait("vector", "sv", v_setup)
            em.wait("scalar", "sv", v_setup)

            v_scan_done = [0, 0]
            v_hc_prev = 0
            v_u_prev = 0
            v_bc_used_par = {0: 0, 1: 0}
            v_bc_ld = {0: 0, 1: 0}
            v_sq_done = [0, 0]
            v_xf_free = [0, 0]
            v_ustore = [0] * (NCHUNK + 2)
            v_bc_used = 0
            v_abuf_free = 0
            pv_dt = pv_bcfix = pv_dtraw = pv_strow = pv_bcrows = pv_xln = 0
            pv_d1 = v_setup
            sq_free = {0: 0, 1: 0}

            for c in range(NCHUNK):
                par = c % 2
                em.wait("sync", "sv", v_scan_done[1 - par])
                v_xo[c] = em.dma("sync", "qs", lambda e, c=c: (
                    a_xo[:, :, :], xo_r[:, :, c * TA:(c + 1) * TA]))
                if c + 1 < NCHUNK:
                    em.wait("gpsimd", "st", v_xf_free[1 - par])
                    em.wait("gpsimd", "sk", v_sq_done[1 - par])
                    v_xf[c + 1] = em.dma("gpsimd", "qg", lambda e, c=c, par=par: (
                        a_xf[:, 1 - par], xf_r[:, :, (c + 1) * TA:(c + 2) * TA]))

                em.wait("scalar", "qg", v_xf[c])
                em.wait("tensor", "qg", v_xf[c])
                em.wait("tensor", "sv", pv_d1)
                em.wait("tensor", "sk", pv_dt)
                for blk in range(NBLK):
                    em.wait("scalar", "st", sq_free[blk % 2])
                    em.op("scalar", lambda e, blk=blk, par=par: e.activation(
                        out=a_sq[:, blk % 2], in_=a_xf[:, par, blk], func=AF.Square),
                        sem="sk")
                    v_sqb = em.c["sk"]
                    for tt in range(2):
                        em.op("tensor", lambda e, blk=blk, par=par, tt=tt: e.matmul(
                            ps[tt][0:1, :], s_ones[:, :],
                            a_xf[:, par, blk, tt * 512:(tt + 1) * 512],
                            start=(blk == 0), stop=(blk == NBLK - 1)))
                    em.wait("tensor", "sk", v_sqb)
                    for tt in range(2):
                        em.op("tensor", lambda e, blk=blk, tt=tt: e.matmul(
                            ps[2 + tt][0:1, :], s_ones[:, :],
                            a_sq[:, blk % 2, tt * 512:(tt + 1) * 512],
                            start=(blk == 0), stop=(blk == NBLK - 1)),
                            sem="st")
                    sq_free[blk % 2] = em.c["st"]
                v_sq_done[par] = em.c["sk"]
                v_stats_mm = em.c["st"]

                for blk in range(NBLK):
                    for tt in range(2):
                        em.op("tensor", lambda e, blk=blk, par=par, tt=tt: e.matmul(
                            ps[4 + tt][0:NST, :], s_wb[:, blk * NST:(blk + 1) * NST],
                            a_xf[:, par, blk, tt * 512:(tt + 1) * 512],
                            start=(blk == 0), stop=(blk == NBLK - 1)))
                        em.op("tensor", lambda e, blk=blk, par=par, tt=tt: e.matmul(
                            ps[6 + tt][0:NST, :], s_wc[:, blk * NST:(blk + 1) * NST],
                            a_xf[:, par, blk, tt * 512:(tt + 1) * 512],
                            start=(blk == 0), stop=(blk == NBLK - 1)),
                            sem="st" if (blk == NBLK - 1 and tt == 1) else None)
                v_bc_mm = em.c["st"]

                em.wait("vector", "st", v_stats_mm)
                em.wait("vector", "qs", pv_strow)
                for tt in range(2):
                    em.op("vector", lambda e, tt=tt: e.tensor_scalar_mul(
                        arow(0, slice(tt * 512, (tt + 1) * 512)), ps[tt][0:1, :], 1.0 / D))
                    em.op("vector", lambda e, tt=tt: e.tensor_scalar_mul(
                        arow(1, slice(tt * 512, (tt + 1) * 512)),
                        ps[2 + tt][0:1, :], 1.0 / D),
                        sem="sv" if tt == 1 else None)
                v_evac = em.c["sv"]
                em.wait("sync", "sv", v_evac)
                em.dma("sync", "qs", lambda e, c=c: (
                    stath[0:1, c * TA:(c + 1) * TA], arow(0)))
                v_strow = em.dma("sync", "qs", lambda e, c=c: (
                    stath[1:2, c * TA:(c + 1) * TA], arow(1)))
                em.wait("gpsimd", "qs", v_strow)
                em.wait("gpsimd", "sv", pv_xln)
                em.wait("gpsimd", "sk", pv_dt)
                em.dma("gpsimd", "qg", lambda e, c=c: (
                    a_mu[:, :],
                    stath[0, c * TA:(c + 1) * TA].unsqueeze(0).partition_broadcast(128)))
                v_bcast = em.dma("gpsimd", "qg", lambda e, c=c: (
                    a_rs[:, :],
                    stath[1, c * TA:(c + 1) * TA].unsqueeze(0).partition_broadcast(128)))
                em.wait("vector", "qg", v_bcast)
                em.op("vector", lambda e: e.tensor_tensor(
                    a_xln[:, 0], a_mu[:, :], a_mu[:, :], OP.mult))
                em.op("vector", lambda e: e.tensor_sub(
                    a_rs[:, :], a_rs[:, :], a_xln[:, 0]), sem="sv")
                v_var = em.c["sv"]
                em.wait("scalar", "sv", v_var)
                em.op("scalar", lambda e: e.activation(
                    out=a_xln[:, 0], in_=a_rs[:, :], func=AF.Ln, bias=s_ct[:, 3:4]))
                em.op("scalar", lambda e: e.activation(
                    out=a_rs[:, :], in_=a_xln[:, 0], func=AF.Exp, scale=s_ct[:, 0:1]),
                    sem="sk")
                v_rstd = em.c["sk"]

                em.wait("vector", "sk", v_rstd)
                em.wait("vector", "st", v_bc_mm)
                em.wait("vector", "qs", pv_bcrows)
                for tt in range(2):
                    sl = slice(tt * 512, (tt + 1) * 512)
                    em.op("vector", lambda e, tt=tt, sl=sl: e.scalar_tensor_tensor(
                        a_Bf[:, sl], a_mu[0:NST, sl], s_nwB[:, 0:1],
                        ps[4 + tt][0:NST, :], OP.mult, OP.add))
                    em.op("vector", lambda e, tt=tt, sl=sl: e.tensor_tensor(
                        a_Bf[:, sl], a_Bf[:, sl], a_rs[0:NST, sl], OP.mult))
                    em.op("vector", lambda e, tt=tt, sl=sl: e.scalar_tensor_tensor(
                        a_Cf[:, sl], a_mu[0:NST, sl], s_nwC[:, 0:1],
                        ps[6 + tt][0:NST, :], OP.mult, OP.add))
                    em.op("vector", lambda e, tt=tt, sl=sl: e.tensor_tensor(
                        a_Cf[:, sl], a_Cf[:, sl], a_rs[0:NST, sl], OP.mult),
                        sem="sv" if tt == 1 else None)
                v_bcfix = em.c["sv"]
                em.wait("sync", "sv", v_bcfix)
                em.dma("sync", "qs", lambda e, c=c: (
                    bch[0][c * NSUB:(c + 1) * NSUB].rearrange(
                        "s (n t) -> n s t", n=NST),
                    a_Bf.rearrange("n (s t) -> n s t", s=NSUB)))
                v_bcrows = em.dma("sync", "qs", lambda e, c=c: (
                    bch[1][c * NSUB:(c + 1) * NSUB].rearrange(
                        "s (n t) -> n s t", n=NST),
                    a_Cf.rearrange("n (s t) -> n s t", s=NSUB)))

                em.wait("tensor", "sv", v_evac)
                for blk in range(NBLK):
                    for tt in range(2):
                        em.op("tensor", lambda e, blk=blk, par=par, tt=tt: e.matmul(
                            ps[tt][0:RDT, :], s_w1[:, blk * RDT:(blk + 1) * RDT],
                            a_xf[:, par, blk, tt * 512:(tt + 1) * 512],
                            start=(blk == 0), stop=(blk == NBLK - 1)),
                            sem="st" if (blk == NBLK - 1 and tt == 1) else None)
                v_d1_mm = em.c["st"]
                v_xf_free[par] = v_d1_mm

                em.wait("vector", "st", v_d1_mm)
                em.wait("vector", "st", pv_dtraw)
                for tt in range(2):
                    sl = slice(tt * 512, (tt + 1) * 512)
                    em.op("vector", lambda e, tt=tt, sl=sl: e.scalar_tensor_tensor(
                        a_sp[0:RDT, 0:512], a_mu[0:RDT, sl], s_nw1[:, 0:1],
                        ps[tt][0:RDT, :], OP.mult, OP.add))
                    em.op("vector", lambda e, tt=tt, sl=sl: e.tensor_tensor(
                        a_d1[:, sl], a_sp[0:RDT, 0:512], a_rs[0:RDT, sl], OP.mult),
                        sem="sv" if tt == 1 else None)
                v_d1 = em.c["sv"]

                em.wait("tensor", "sv", v_d1)
                for blk in range(2):
                    for tt in range(2):
                        em.op("tensor", lambda e, blk=blk, tt=tt: e.matmul(
                            ps[2 + 2 * blk + tt][:, :],
                            s_w2[:, blk * 128:(blk + 1) * 128],
                            a_d1[:, tt * 512:(tt + 1) * 512],
                            start=True, stop=True),
                            sem="st" if (blk == 1 and tt == 1) else None)
                v_dtraw = em.c["st"]
                em.wait("scalar", "st", v_dtraw)
                for blk in range(2):
                    for tt in range(2):
                        em.op("scalar", lambda e, blk=blk, tt=tt: e.activation(
                            out=a_sp[:, :], in_=ps[2 + 2 * blk + tt][:, :],
                            func=AF.Exp, bias=s_bd[:, blk:blk + 1]))
                        em.op("scalar", lambda e, blk=blk, tt=tt: e.activation(
                            out=a_dt[:, blk, tt * 512:(tt + 1) * 512], in_=a_sp[:, :],
                            func=AF.Ln, bias=s_ct[:, 1:2]),
                            sem="sk" if (blk == 1 and tt == 1) else None)
                v_dt = em.c["sk"]

                em.wait("vector", "sk", v_rstd)
                em.wait("vector", "qs", v_xo[c])
                for blk in range(2):
                    em.op("vector", lambda e, blk=blk, par=par: e.tensor_sub(
                        a_xln[:, blk], a_xo[:, blk], a_mu[:, :]))
                    em.op("vector", lambda e, blk=blk: e.tensor_tensor(
                        a_xln[:, blk], a_xln[:, blk], a_rs[:, :], OP.mult),
                        sem="sv" if blk == 1 else None)
                v_xln = em.c["sv"]

                def bc_load(cs, pa):
                    em.wait("gpsimd", "qs", v_bcrows)
                    em.wait("gpsimd", "sv", v_bc_used_par[pa])
                    em.dma("gpsimd", "qg", lambda e, cs=cs, pa=pa: (
                        a_Bb[:, pa], bch[0, cs].unsqueeze(0).partition_broadcast(128)))
                    return em.dma("gpsimd", "qg", lambda e, cs=cs, pa=pa: (
                        a_Cb[:, pa], bch[1, cs].unsqueeze(0).partition_broadcast(128)))

                for s in range(NSUB):
                    t0 = s * TC
                    pa = s % 2
                    if s == 0:
                        v_bcbc = bc_load(c * NSUB, 0)
                        v_bc_ld[0] = v_bcbc
                    else:
                        v_bcbc = v_bc_ld[pa]
                    if s + 1 < NSUB:
                        v_bc_ld[1 - pa] = bc_load(c * NSUB + s + 1, 1 - pa)
                    for blk in range(2):
                        em.wait("scalar", "sk", v_dt)
                        em.wait("scalar", "sv", v_abuf_free)
                        for n in range(NST):
                            em.op("scalar", lambda e, blk=blk, n=n, t0=t0: e.activation(
                                out=a_ab[:, n * TC:(n + 1) * TC],
                                in_=a_dt[:, blk, t0:t0 + TC], func=AF.Exp,
                                scale=s_A[:, blk * NST + n:blk * NST + n + 1]),
                                sem="sk" if n == NST - 1 else None)
                        v_exp = em.c["sk"]

                        b03 = a_b0.rearrange("p (n t) -> p n t", n=NST)
                        em.op("vector", lambda e, blk=blk, t0=t0, b03=b03: e.tensor_tensor(
                            b03,
                            a_xln[:, blk, t0:t0 + TC].unsqueeze(1)
                            .broadcast_to([128, NST, TC]),
                            s_rA[:, blk * NST:(blk + 1) * NST].unsqueeze(2)
                            .broadcast_to([128, NST, TC]),
                            OP.mult))
                        em.wait("vector", "qg", v_bcbc)
                        em.op("vector", lambda e, pa=pa: e.tensor_tensor(
                            a_b0[:, :], a_b0[:, :], a_Bb[:, pa], OP.mult), sem="sv")
                        em.wait("vector", "sv", em.c["sv"])
                        em.wait("vector", "sk", v_exp)
                        em.op("vector", lambda e: e.scalar_tensor_tensor(
                            a_bb[:, :], a_ab[:, :], 1.0, a_b0[:, :],
                            OP.subtract, OP.mult))
                        first = (c == 0 and s == 0)
                        for n in range(NST):
                            init = 0.0 if first else a_hs[:, blk * NST + n:blk * NST + n + 1]
                            em.op("vector", lambda e, n=n, init=init: e.tensor_tensor_scan(
                                a_hh[:, n * TC:(n + 1) * TC],
                                a_ab[:, n * TC:(n + 1) * TC],
                                a_bb[:, n * TC:(n + 1) * TC],
                                init, OP.mult, OP.add),
                                sem="sv" if n == NST - 1 else None)
                        v_abuf_free = em.c["sv"]
                        em.wait("vector", "sv", v_abuf_free)
                        hh3 = a_hh.rearrange("p (n t) -> p n t", n=NST)
                        em.op("vector", lambda e, blk=blk, hh3=hh3: e.tensor_copy(
                            a_hs[:, blk * NST:(blk + 1) * NST], hh3[:, :, TC - 1]),
                            sem="sv")
                        v_hs = em.c["sv"]
                        em.wait("vector", "sv", v_hs)
                        em.op("vector", lambda e, pa=pa: e.tensor_tensor(
                            a_b0[:, :], a_hh[:, :], a_Cb[:, pa], OP.mult), sem="sv")
                        em.wait("vector", "sv", em.c["sv"])
                        hcT = a_b0.rearrange("p (n t) -> p n t", n=NST).transpose([0, 2, 1])
                        em.op("vector", lambda e, hcT=hcT: e.tensor_reduce(
                            a_y[:, :].unsqueeze(2), hcT, AX.X, OP.add))
                        em.wait("vector", "qs", v_ustore[c])
                        em.op("vector", lambda e, blk=blk, par=par, t0=t0:
                              e.scalar_tensor_tensor(
                                  a_u[:, par, blk, t0:t0 + TC],
                                  a_xln[:, blk, t0:t0 + TC], s_ds[:, blk:blk + 1],
                                  a_xo[:, blk, t0:t0 + TC], OP.mult, OP.add))
                        em.op("vector", lambda e, blk=blk, par=par, t0=t0: e.tensor_add(
                            a_u[:, par, blk, t0:t0 + TC],
                            a_u[:, par, blk, t0:t0 + TC], a_y[:, :]), sem="sv")
                        v_u_prev = em.c["sv"]
                        if blk == 1:
                            v_bc_used_par[pa] = em.c["sv"]
                        if stage == "A" and c == 0 and s == 0 and blk == 0:
                            v_dump = em.c["sv"]
                            em.wait("gpsimd", "sv", v_dump)
                            em.dma("gpsimd", "qg", lambda e: (dbg_dt[:, :, :], a_dt[:, :, :]))
                            em.dma("gpsimd", "qg", lambda e: (dbg_xln[:, :, :], a_xln[:, :, :]))
                            em.dma("gpsimd", "qg", lambda e: (dbg_rs[:, :], a_rs[:, :]))
                            em.dma("gpsimd", "qg", lambda e: (dbg_mu[:, :], a_mu[:, :]))
                            em.dma("gpsimd", "qg", lambda e: (dbg_B[:, :], a_Bf[:, :]))
                            em.dma("gpsimd", "qg", lambda e: (dbg_ab[:, :], a_ab[:, :]))
                            em.dma("gpsimd", "qg", lambda e: (dbg_bb[:, :], a_bb[:, :]))
                            em.dma("gpsimd", "qg", lambda e: (dbg_hh[:, :], a_hh[:, :]))
                            em.dma("gpsimd", "qg", lambda e: (dbg_A[:, :], s_A[:, :]))
                            em.dma("gpsimd", "qg", lambda e: (dbg_hs[:, :], a_hs[:, :]))
                            em.dma("gpsimd", "qg", lambda e: (dbg_rA[:, :], s_rA[:, :]))
                            em.dma("gpsimd", "qg", lambda e: (dbg_Cb[:, :], a_Cb[:, 0]))
                            em.dma("gpsimd", "qg", lambda e: (dbg_C[:, :], a_Cf[:, :]))
                            em.dma("gpsimd", "qg", lambda e: (dbg_y[:, :], a_y[:, :]))
                            em.dma("gpsimd", "qg", lambda e: (dbg_hc[:, :], a_b0[:, :]))
                            v_dumped = em.c["qg"]
                            for en in ("vector", "scalar", "tensor"):
                                em.wait(en, "qg", v_dumped)
                    v_bc_used = em.c["sv"]
                v_scan_done[par] = em.c["sv"]
                em.wait("sync", "sv", v_scan_done[par])
                v_ustore[c + 2] = em.dma("sync", "qs", lambda e, c=c, par=par: (
                    su_r[:, :, c * TA:(c + 1) * TA], a_u[:, par]))

                pv_d1, pv_dt, pv_bcfix = v_d1, v_dt, v_bcfix
                pv_dtraw, pv_strow, pv_bcrows, pv_xln = v_dtraw, v_strow, v_bcrows, v_xln

            v_all_u = em.c["qs"]
            if stage == "A":
                em.wait("sync", "qs", v_all_u)
                em.wait("gpsimd", "qs", v_all_u)
                return

            em.wait("gpsimd", "qs", v_all_u)
            em.op("gpsimd", lambda e: e.collective_compute(
                "AllGather", OP.bypass,
                replica_groups=[[0, 1, 2, 3], [4, 5, 6, 7]],
                ins=[sendu[:, :]], outs=[recvu[:, :]]), sem="cc")
            v_cc = em.c["cc"]
            for en in ENGINES:
                em.wait(en, "cc", v_cc)

            v_cu = em.dma("sync", "qs", lambda e: (
                c_u[:, :, :], rv_r[:, :, bass.ts(em.q(), TQ)]))
            if stage == "B":
                em.wait("sync", "qs", v_cu)
                v_outB = em.dma("sync", "qs", lambda e: (out_r[:, :, :], c_u[:, :, :]))
                em.wait("sync", "qs", v_outB)
                em.wait("gpsimd", "qs", v_outB)
                return


            def _cut(n, waits):
                if stage == "C1" and cut == n:
                    for en, sem, val in waits:
                        em.wait(en, sem, val)
                    vo = em.dma("sync", "qs", lambda e: (out_r[:, :, :], c_u[:, :, :]))
                    em.wait("sync", "qs", vo)
                    em.wait("gpsimd", "qs", vo)
                    return True
                return False

            if _cut(0, [("sync", "qs", v_cu)]):
                return

            ub_free = {0: 0, 1: 0}
            csq_free = {0: 0, 1: 0}
            for blk in range(NBLK):
                em.wait("gpsimd", "st", ub_free[blk % 2])
                v_ub = em.dma("gpsimd", "qg", lambda e, blk=blk: (
                    c_ub[:, blk % 2], rv_r[:, blk, bass.ts(em.q(), TQ)]))
                em.wait("scalar", "qg", v_ub)
                em.wait("scalar", "st", csq_free[blk % 2])
                em.op("scalar", lambda e, blk=blk: e.activation(
                    out=c_sq[:, blk % 2], in_=c_ub[:, blk % 2], func=AF.Square),
                    sem="sk")
                v_sq2 = em.c["sk"]
                em.wait("tensor", "qg", v_ub)
                for tt in range(4):
                    em.op("tensor", lambda e, blk=blk, tt=tt: e.matmul(
                        ps[tt][0:1, :], s_ones[:, :],
                        c_ub[:, blk % 2, tt * 512:(tt + 1) * 512],
                        start=(blk == 0), stop=(blk == NBLK - 1)),
                        sem="st" if tt == 3 else None)
                ub_free[blk % 2] = em.c["st"]
                em.wait("tensor", "sk", v_sq2)
                for tt in range(4):
                    em.op("tensor", lambda e, blk=blk, tt=tt: e.matmul(
                        ps[4 + tt][0:1, :], s_ones[:, :],
                        c_sq[:, blk % 2, tt * 512:(tt + 1) * 512],
                        start=(blk == 0), stop=(blk == NBLK - 1)),
                        sem="st" if tt == 3 else None)
                csq_free[blk % 2] = em.c["st"]
            v_stat2 = em.c["st"]
            em.wait("vector", "st", v_stat2)
            for tt in range(4):
                sl = slice(tt * 512, (tt + 1) * 512)
                em.op("vector", lambda e, tt=tt, sl=sl: e.tensor_scalar_mul(
                    crow(0, sl), ps[tt][0:1, :], 1.0 / D))
                em.op("vector", lambda e, tt=tt, sl=sl: e.tensor_scalar_mul(
                    crow(1, sl), ps[4 + tt][0:1, :], 1.0 / D),
                    sem="sv" if tt == 3 else None)
            v_ev2 = em.c["sv"]
            if _cut(1, [("sync", "sv", v_ev2), ("sync", "qs", v_cu)]):
                return
            em.wait("sync", "sv", v_ev2)
            em.dma("sync", "qs", lambda e: (statc[0:1, :], crow(0)))
            v_str2 = em.dma("sync", "qs", lambda e: (statc[1:2, :], crow(1)))
            em.wait("gpsimd", "qs", v_str2)
            em.dma("gpsimd", "qg", lambda e: (
                c_mu[:, :], statc[0, :].unsqueeze(0).partition_broadcast(128)))
            v_bc2 = em.dma("gpsimd", "qg", lambda e: (
                c_v2[:, :], statc[1, :].unsqueeze(0).partition_broadcast(128)))
            em.wait("vector", "qg", v_bc2)
            em.op("vector", lambda e: e.tensor_tensor(
                c_t32[:, :], c_mu[:, :], c_mu[:, :], OP.mult))
            em.op("vector", lambda e: e.tensor_sub(
                c_v2[:, :], c_v2[:, :], c_t32[:, :]), sem="sv")
            v_var2 = em.c["sv"]
            em.wait("scalar", "sv", v_var2)
            em.op("scalar", lambda e: e.activation(
                out=c_t32[:, :], in_=c_v2[:, :], func=AF.Ln, bias=s_ct[:, 3:4]))
            em.op("scalar", lambda e: e.activation(
                out=c_t32[:, :], in_=c_t32[:, :], func=AF.Exp, scale=s_ct[:, 0:1]),
                sem="sk")
            v_r1 = em.c["sk"]
            em.wait("vector", "sk", v_r1)
            em.op("vector", lambda e: e.tensor_tensor(
                c_v2[:, :], c_v2[:, :], c_t32[:, :], OP.mult))
            em.op("vector", lambda e: e.tensor_tensor(
                c_v2[:, :], c_v2[:, :], c_t32[:, :], OP.mult), sem="sv")
            v_v2 = em.c["sv"]
            em.wait("scalar", "sv", v_v2)
            em.op("scalar", lambda e: e.activation(
                out=c_v2[:, :], in_=c_v2[:, :], func=AF.Ln, bias=s_ct[:, 3:4]))
            em.op("scalar", lambda e: e.activation(
                out=c_v2[:, :], in_=c_v2[:, :], func=AF.Exp, scale=s_ct[:, 0:1]),
                sem="sk")
            v_r2 = em.c["sk"]
            em.wait("vector", "sk", v_r2)
            em.op("vector", lambda e: e.tensor_tensor(
                c_al[:, :], c_t32[:, :], c_v2[:, :], OP.mult), sem="sv")
            v_alpha = em.c["sv"]
            if _cut(2, [("sync", "sv", v_alpha), ("sync", "qs", v_cu)]):
                return

            v_nb_blk = [0] * NBLK
            em.wait("vector", "sv", v_alpha)
            em.wait("vector", "qs", v_cu)
            for blk in range(NBLK):
                em.op("vector", lambda e, blk=blk: e.tensor_sub(
                    c_t32[:, :], c_u[:, blk], c_mu[:, :]))
                em.op("vector", lambda e: e.tensor_tensor(
                    c_t32[:, :], c_t32[:, :], c_al[:, :], OP.mult))
                em.op("vector", lambda e, blk=blk: e.tensor_scalar(
                    c_nb[:, blk], c_t32[:, :], s_gm[:, blk:blk + 1],
                    s_bt[:, blk:blk + 1], OP.mult, OP.add), sem="sv")
                v_nb_blk[blk] = em.c["sv"]
            v_normed = em.c["sv"]
            if _cut(3, [("sync", "sv", v_normed)]):
                return

            for blk in range(NBLK):
                em.wait("tensor", "sv", v_nb_blk[blk])
                for tt in range(4):
                    em.op("tensor", lambda e, blk=blk, tt=tt: e.matmul(
                        ps[tt][0:E, :], s_rt[:, blk * E:(blk + 1) * E],
                        c_nb[:, blk, tt * 512:(tt + 1) * 512],
                        start=(blk == 0), stop=(blk == NBLK - 1)),
                        sem="st" if (blk == NBLK - 1 and tt == 3) else None)
            v_logit = em.c["st"]
            em.wait("vector", "st", v_logit)
            for tt in range(4):
                em.op("vector", lambda e, tt=tt: e.tensor_copy(
                    c_lg[:, tt * 512:(tt + 1) * 512], ps[tt][0:E, :]),
                    sem="sv" if tt == 3 else None)
            v_lgd = em.c["sv"]
            if _cut(4, [("sync", "sv", v_lgd)]):
                return
            em.wait("tensor", "sv", v_lgd)
            for ti in range(16):
                em.op("tensor", lambda e, ti=ti: e.transpose(
                    ps[4][:, ti * E:(ti + 1) * E],
                    c_lg[:, ti * 128:(ti + 1) * 128], s_eye[0:E, 0:E]),
                    sem="st" if ti == 15 else None)
            v_lgT = em.c["st"]
            em.wait("vector", "st", v_lgT)
            em.op("vector", lambda e: e.tensor_copy(c_lgT[:, :], ps[4][:, 0:128]))
            for ti in range(16):
                em.op("vector", lambda e, ti=ti: e.tensor_reduce(
                    c_sc[:, ti:ti + 1], c_lgT[:, ti * E:(ti + 1) * E], AX.X, OP.max))
            em.op("vector", lambda e: e.tensor_scalar_mul(
                c_sc[:, 0:16], c_sc[:, 0:16], -1.0), sem="sv")
            v_nm = em.c["sv"]
            em.wait("vector", "sv", v_nm)
            em.wait("scalar", "sv", v_nm)
            for ti in range(16):
                em.op("scalar", lambda e, ti=ti: e.activation(
                    out=c_ket[:, ti * E:(ti + 1) * E], in_=c_lgT[:, ti * E:(ti + 1) * E],
                    func=AF.Exp, bias=c_sc[:, ti:ti + 1]),
                    sem="sk" if ti == 15 else None)
            v_ket = em.c["sk"]
            em.wait("vector", "sk", v_ket)
            for ti in range(16):
                em.op("vector", lambda e, ti=ti: e.tensor_reduce(
                    c_sc[:, 16 + ti:17 + ti], c_ket[:, ti * E:(ti + 1) * E], AX.X, OP.add))
            for ti in range(16):
                em.op("vector", lambda e, ti=ti: e.tensor_reduce(
                    c_sc[:, 32 + ti:33 + ti], c_ket[:, ti * E:(ti + 1) * E], AX.X, OP.max),
                    sem="sv" if ti == 15 else None)
            v_zs = em.c["sv"]
            em.wait("vector", "sv", v_zs)
            for ti in range(16):
                em.op("vector", lambda e, ti=ti: e.tensor_scalar(
                    c_m1[:, ti * E:(ti + 1) * E], c_ket[:, ti * E:(ti + 1) * E],
                    c_sc[:, 32 + ti:33 + ti], None, OP.is_ge),
                    sem="sv" if ti == 15 else None)
            em.wait("vector", "sv", em.c["sv"])
            for ti in range(16):
                em.op("vector", lambda e, ti=ti: e.scalar_tensor_tensor(
                    c_md[:, ti * E:(ti + 1) * E], c_m1[:, ti * E:(ti + 1) * E],
                    -1e30, c_ket[:, ti * E:(ti + 1) * E], OP.mult, OP.add),
                    sem="sv" if ti == 15 else None)
            em.wait("vector", "sv", em.c["sv"])
            for ti in range(16):
                em.op("vector", lambda e, ti=ti: e.tensor_reduce(
                    c_sc[:, 48 + ti:49 + ti], c_md[:, ti * E:(ti + 1) * E], AX.X, OP.max),
                    sem="sv" if ti == 15 else None)
            v_v2s = em.c["sv"]
            em.wait("vector", "sv", v_v2s)
            em.op("vector", lambda e: e.tensor_sub(
                c_sc[:, 64:80], c_sc[:, 32:48], c_sc[:, 48:64]), sem="sv")
            em.wait("vector", "sv", em.c["sv"])
            em.op("vector", lambda e: e.reciprocal(c_sc[:, 96:112], c_sc[:, 16:32]),
                  sem="sv")
            em.wait("vector", "sv", em.c["sv"])
            em.op("vector", lambda e: e.tensor_tensor(
                c_sc[:, 64:80], c_sc[:, 64:80], c_sc[:, 96:112], OP.mult), sem="sv")
            em.wait("vector", "sv", em.c["sv"])
            v_t = em.c["sv"]
            em.wait("scalar", "sv", v_t)
            em.op("scalar", lambda e: e.activation(
                out=c_sc[:, 80:96], in_=c_sc[:, 64:80], func=AF.Exp, scale=s_ct[:, 2:3]),
                sem="sk")
            v_en = em.c["sk"]
            em.wait("vector", "sk", v_en)
            em.op("vector", lambda e: e.tensor_scalar_add(
                c_sc[:, 96:112], c_sc[:, 80:96], 1.0), sem="sv")
            em.wait("vector", "sv", em.c["sv"])
            em.op("vector", lambda e: e.reciprocal(c_sc[:, 64:80], c_sc[:, 96:112]),
                  sem="sv")
            em.wait("vector", "sv", em.c["sv"])
            em.op("vector", lambda e: e.tensor_scalar(
                c_sc[:, 80:96], c_sc[:, 64:80], -1.0, 1.0, OP.mult, OP.add), sem="sv")
            em.wait("vector", "sv", em.c["sv"])
            v_g = em.c["sv"]
            em.wait("vector", "sv", v_g)
            for ti in range(16):
                em.op("vector", lambda e, ti=ti: e.tensor_scalar(
                    c_cmb[:, ti * E:(ti + 1) * E], c_m1[:, ti * E:(ti + 1) * E],
                    c_sc[:, 64 + ti:65 + ti], None, OP.mult),
                    sem="sv" if ti == 15 else None)
            em.wait("vector", "sv", em.c["sv"])
            for ti in range(16):
                em.op("vector", lambda e, ti=ti: e.tensor_scalar(
                    c_m1[:, ti * E:(ti + 1) * E], c_md[:, ti * E:(ti + 1) * E],
                    c_sc[:, 48 + ti:49 + ti], None, OP.is_ge),
                    sem="sv" if ti == 15 else None)
            em.wait("vector", "sv", em.c["sv"])
            for ti in range(16):
                em.op("vector", lambda e, ti=ti: e.scalar_tensor_tensor(
                    c_cmb[:, ti * E:(ti + 1) * E], c_m1[:, ti * E:(ti + 1) * E],
                    c_sc[:, 80 + ti:81 + ti], c_cmb[:, ti * E:(ti + 1) * E],
                    OP.mult, OP.add), sem="sv" if ti == 15 else None)
            v_cmb = em.c["sv"]
            if _cut(5, [("sync", "sv", v_cmb)]):
                return
            em.wait("tensor", "sv", v_cmb)
            v_cpg = 0
            for g in range(4):
                em.wait("tensor", "sv", v_cpg)
                for i in range(4):
                    ti = g * 4 + i
                    em.op("tensor", lambda e, ti=ti, i=i: e.transpose(
                        ps[5][0:E, i * 128:(i + 1) * 128],
                        c_cmb[:, ti * E:(ti + 1) * E], s_eye[:, :]),
                        sem="st" if i == 3 else None)
                v_tr = em.c["st"]
                em.wait("vector", "st", v_tr)
                em.op("vector", lambda e, g=g: e.tensor_copy(
                    c_lg[:, g * 512:(g + 1) * 512], ps[5][0:E, 0:512]),
                    sem="sv")
                v_cpg = em.c["sv"]
            v_cT = em.c["sv"]
            em.wait("sync", "sv", v_cT)
            v_combh = em.dma("sync", "qs", lambda e: (combh[:, :], c_lg[:, :]))
            if stage == "C2":
                em.wait("gpsimd", "qs", v_combh)
                em.dma("gpsimd", "qg", lambda e: (out_r[:, 0, 0:128], c_lgT[:, :]))
                em.dma("gpsimd", "qg", lambda e: (out_r[:, 1, 0:128], c_ket[:, :]))
                em.dma("gpsimd", "qg", lambda e: (out_r[:, 2, 0:112], c_sc[:, :]))
                em.dma("gpsimd", "qg", lambda e: (out_r[:, 3, 0:128], c_m1[:, :]))
                em.dma("gpsimd", "qg", lambda e: (out_r[:, 4, 0:128], c_md[:, :]))
                em.dma("gpsimd", "qg", lambda e: (out_r[:, 5, 0:128], c_cmb[:, :]))
                vd = em.dma("gpsimd", "qg", lambda e: (
                    out_r[:, 6, 0:TQ],
                    combh[0, :].unsqueeze(0).partition_broadcast(128)))
                em.wait("gpsimd", "qg", vd)
                em.wait("sync", "qs", em.c["qs"])
                return
            if _cut(6, [("sync", "qs", v_combh)]):
                return

            if stage == "C1":
                em.wait("gpsimd", "sv", v_normed)
                em.wait("gpsimd", "qs", v_combh)
                v_o1 = em.dma("gpsimd", "qg", lambda e: (out_r[:, :, :], c_nb[:, :, :]))
                em.wait("gpsimd", "qg", v_o1)
                em.wait("sync", "qs", em.c["qs"])
                return

            gu_free_sk = {0: 0, 1: 0}
            gu_free_sv = {0: v_lgd, 1: v_cT}
            csg_free = {0: 0, 1: 0}
            dn_free = {0: 0, 1: 0}
            v_wg_free = {0: 0, 1: 0}
            v_wd_free = {0: 0, 1: 0}
            v_dn_mm_all = 0
            v_cb_used = 0
            v_out = 0

            elist = ([E] + list(range(E))) if experts is None else list(experts)
            for e_i in elist:
                routed = e_i < E
                NM = 24 if routed else 8
                NF = 24 if routed else 8
                if routed:
                    em.wait("gpsimd", "qs", v_combh)
                    em.wait("gpsimd", "sv", v_cb_used)
                    v_cb = em.dma("gpsimd", "qg", lambda e, e_i=e_i: (
                        c_cb[:, :],
                        combh[e_i, :].unsqueeze(0).partition_broadcast(128)))
                else:
                    v_cb = 0
                for th in range(2):
                    tof = th * 1024
                    v_gmm = {}
                    v_umm = {}
                    v_sil = {}
                    v_h = {}
                    for mg in range(NM):
                        pw = mg % 2
                        em.wait("gpsimd", "st", v_wg_free[pw])
                        if routed:
                            em.dma("gpsimd", "qg", lambda e, e_i=e_i, mg=mg, pw=pw: (
                                c_wg[:, pw], guN[e_i, mg]))
                            v_w = em.dma("gpsimd", "qg", lambda e, e_i=e_i, mg=mg, pw=pw: (
                                c_wu[:, pw], guN[e_i, 24 + mg]))
                        else:
                            em.dma("gpsimd", "qg", lambda e, mg=mg, pw=pw: (
                                c_wg[:, pw], sgN[mg]))
                            v_w = em.dma("gpsimd", "qg", lambda e, mg=mg, pw=pw: (
                                c_wu[:, pw], sgN[8 + mg]))
                        em.wait("tensor", "qg", v_w)
                        em.wait("tensor", "sk", gu_free_sk[pw])
                        em.wait("tensor", "sv", gu_free_sv[pw])
                        for k in range(NBLK):
                            for tt in range(2):
                                em.op("tensor", lambda e, k=k, tt=tt, pw=pw, tof=tof:
                                      e.matmul(
                                          ps[4 * pw + tt][:, :],
                                          c_wg[:, pw, k * 128:(k + 1) * 128],
                                          c_nb[:, k, tof + tt * 512:tof + (tt + 1) * 512],
                                          start=(k == 0), stop=(k == NBLK - 1)),
                                      sem="st" if (k == NBLK - 1 and tt == 1) else None)
                        v_gmm[mg] = em.c["st"]
                        for k in range(NBLK):
                            for tt in range(2):
                                em.op("tensor", lambda e, k=k, tt=tt, pw=pw, tof=tof:
                                      e.matmul(
                                          ps[4 * pw + 2 + tt][:, :],
                                          c_wu[:, pw, k * 128:(k + 1) * 128],
                                          c_nb[:, k, tof + tt * 512:tof + (tt + 1) * 512],
                                          start=(k == 0), stop=(k == NBLK - 1)),
                                      sem="st" if (k == NBLK - 1 and tt == 1) else None)
                        v_umm[mg] = em.c["st"]
                        v_wg_free[pw] = v_umm[mg]
                        em.wait("scalar", "st", v_gmm[mg])
                        em.wait("scalar", "sv", csg_free[pw])
                        for tt in range(2):
                            em.op("scalar", lambda e, tt=tt, pw=pw: e.activation(
                                out=c_sg[:, pw, tt * 512:(tt + 1) * 512],
                                in_=ps[4 * pw + tt][:, :], func=AF.Silu),
                                sem="sk" if tt == 1 else None)
                        v_sil[mg] = em.c["sk"]
                        gu_free_sk[pw] = v_sil[mg]
                        em.wait("vector", "st", v_umm[mg])
                        em.wait("vector", "sk", v_sil[mg])
                        em.wait("vector", "st", v_dn_mm_all)
                        if routed and not no_comb:
                            em.wait("vector", "qg", v_cb)
                            for tt in range(2):
                                em.op("vector", lambda e, tt=tt, pw=pw, tof=tof:
                                      e.tensor_tensor(
                                          c_uc[:, tt * 512:(tt + 1) * 512],
                                          ps[4 * pw + 2 + tt][:, :],
                                          c_cb[:, tof + tt * 512:tof + (tt + 1) * 512],
                                          OP.mult))
                                em.op("vector", lambda e, tt=tt, mg=mg, pw=pw:
                                      e.tensor_tensor(
                                          c_hb[:, mg, tt * 512:(tt + 1) * 512],
                                          c_sg[:, pw, tt * 512:(tt + 1) * 512],
                                          c_uc[:, tt * 512:(tt + 1) * 512], OP.mult),
                                      sem="sv" if tt == 1 else None)
                        else:
                            for tt in range(2):
                                em.op("vector", lambda e, tt=tt, mg=mg, pw=pw:
                                      e.tensor_tensor(
                                          c_hb[:, mg, tt * 512:(tt + 1) * 512],
                                          c_sg[:, pw, tt * 512:(tt + 1) * 512],
                                          ps[4 * pw + 2 + tt][:, :], OP.mult),
                                      sem="sv" if tt == 1 else None)
                        v_h[mg] = em.c["sv"]
                        gu_free_sv[pw] = v_h[mg]
                        csg_free[pw] = v_h[mg]
                    v_hall = v_h[NM - 1]
                    v_cb_used = v_hall
                    for md in range(0 if no_down else NBLK):
                        pw2 = md % 2
                        em.wait("gpsimd", "st", v_wd_free[pw2])
                        if routed:
                            v_wd = em.dma("gpsimd", "qg", lambda e, e_i=e_i, md=md, pw2=pw2:
                                          (c_wd[:, pw2, 0:NF * 128], dnN[e_i, md]))
                        else:
                            v_wd = em.dma("gpsimd", "qg", lambda e, md=md, pw2=pw2:
                                          (c_wd[:, pw2, 0:NF * 128], sdN[md]))
                        em.wait("tensor", "qg", v_wd)
                        em.wait("tensor", "sv", v_hall)
                        em.wait("tensor", "sv", dn_free[pw2])
                        em.wait("tensor", "sk", gu_free_sk[0])
                        em.wait("tensor", "sv", gu_free_sv[0])
                        for f in range(NF):
                            for tt in range(2):
                                em.op("tensor", lambda e, f=f, tt=tt, pw2=pw2: e.matmul(
                                    ps[2 * pw2 + tt][:, :],
                                    c_wd[:, pw2, f * 128:(f + 1) * 128],
                                    c_hb[:, f, tt * 512:(tt + 1) * 512],
                                    start=(f == 0), stop=(f == NF - 1)),
                                    sem="st" if (f == NF - 1 and tt == 1) else None)
                        v_dmm = em.c["st"]
                        v_wd_free[pw2] = v_dmm
                        v_dn_mm_all = v_dmm
                        em.wait("vector", "st", v_dmm)
                        for tt in range(2):
                            em.op("vector", lambda e, md=md, tt=tt, pw2=pw2, tof=tof:
                                  e.scalar_tensor_tensor(
                                      c_u[:, md, tof + tt * 512:tof + (tt + 1) * 512],
                                      ps[2 * pw2 + tt][:, :], 1.0,
                                      c_u[:, md, tof + tt * 512:tof + (tt + 1) * 512],
                                      OP.mult, OP.add),
                                  sem="sv" if tt == 1 else None)
                        dn_free[pw2] = em.c["sv"]
                    gu_free_sv[0] = max(gu_free_sv[0], dn_free[0], dn_free[1])

            v_fin = max(dn_free[0], dn_free[1])
            em.wait("sync", "sv", v_fin)
            v_out = em.dma("sync", "qs", lambda e: (out_r[:, :, :], c_u[:, :, :]))
            em.wait("sync", "qs", v_out)
            em.wait("gpsimd", "qs", v_out)

        @block.tensor
        def _(eng):
            program(Em("tensor", eng, sems))

        @block.vector
        def _(eng):
            program(Em("vector", eng, sems))

        @block.scalar
        def _(eng):
            program(Em("scalar", eng, sems))

        @block.gpsimd
        def _(eng):
            program(Em("gpsimd", eng, sems))

        @block.sync
        def _(eng):
            program(Em("sync", eng, sems))

    return nc


def _prep_host(inputs):
    f32 = np.float32
    x = np.asarray(inputs["L0_attn_x"], dtype=f32)
    W_B = np.asarray(inputs["L0_attn_W_B"], dtype=f32)
    W_C = np.asarray(inputs["L0_attn_W_C"], dtype=f32)
    W_dt1 = np.asarray(inputs["L0_attn_W_dt1"], dtype=f32)
    W_dt2 = np.asarray(inputs["L0_attn_W_dt2"], dtype=f32)
    b_dt2 = np.asarray(inputs["L0_attn_b_dt2"], dtype=f32)
    log_A = np.asarray(inputs["L0_attn_log_A"], dtype=f32)
    D_skip = np.asarray(inputs["L0_attn_D_skip"], dtype=f32)
    gam = np.asarray(inputs["L0_ffn_ln_gamma"], dtype=f32)
    bet = np.asarray(inputs["L0_ffn_ln_beta"], dtype=f32)
    rw = np.asarray(inputs["L0_ffn_router_w"], dtype=f32)
    guw = np.asarray(inputs["L0_ffn_gate_up_w"], dtype=f32)
    dnw = np.asarray(inputs["L0_ffn_down_w"], dtype=f32)
    sgw = np.asarray(inputs["L0_ffn_shared_gate_up_w"], dtype=f32)
    sdw = np.asarray(inputs["L0_ffn_shared_down_w"], dtype=f32)

    ac = np.ascontiguousarray
    xT = [ac(x[b].T) for b in range(2)]

    def blkcols(w):
        m = w.shape[1]
        return ac(w.reshape(NBLK, 128, m).transpose(1, 0, 2).reshape(128, NBLK * m))

    wbN = blkcols(ac(W_B.T))
    wcN = blkcols(ac(W_C.T))
    w1N = blkcols(ac(W_dt1.T))
    rtN = blkcols(ac(rw.T))
    gmN = ac(gam.reshape(NBLK, 128).T)
    btN = ac(bet.reshape(NBLK, 128).T)
    eyN = np.eye(128, dtype=f32)
    ctN = np.tile(np.array([[-0.5, 1.0, -1.0, LN_EPS]], dtype=f32), (128, 1))

    guT = guw.transpose(0, 2, 1)
    guN = ac(guT.reshape(E, NBLK, 128, 48, 128)
             .transpose(0, 3, 2, 1, 4).reshape(E, 48, 128, NBLK * 128))
    dnT = dnw.transpose(0, 2, 1)
    dnN = ac(dnT.reshape(E, 24, 128, NBLK, 128)
             .transpose(0, 3, 2, 1, 4).reshape(E, NBLK, 128, 24 * 128))
    sgT = ac(sgw.T)
    sgN = ac(sgT.reshape(NBLK, 128, 16, 128)
             .transpose(2, 1, 0, 3).reshape(16, 128, NBLK * 128))
    sdT = ac(sdw.T)
    sdN = ac(sdT.reshape(NBLK, 128, NBLK, 128)
             .transpose(2, 1, 0, 3).reshape(NBLK, 128, NBLK * 128))

    in_maps = []
    for k in range(8):
        b, qq = divmod(k, 4)
        sl = slice(qq * CH, (qq + 1) * CH)
        in_maps.append({
            "xf": xT[b],
            "xo": ac(xT[b][sl]),
            "wbN": wbN, "wcN": wcN, "w1N": w1N,
            "w2N": ac(W_dt2.T[:, sl]),
            "bdN": ac(b_dt2[sl].reshape(2, 128).T),
            "laN": ac(log_A[sl].reshape(2, 128, NST).transpose(1, 0, 2)
                      .reshape(128, 2 * NST)),
            "dsN": ac(D_skip[sl].reshape(2, 128).T),
            "gmN": gmN, "btN": btN, "rtN": rtN, "eyN": eyN, "ctN": ctN,
            "guN": guN, "dnN": dnN, "sgN": sgN, "sdN": sdN,
        })
    return in_maps


_NC_CACHE = {}


def kernel(**inputs):
    in_maps = _prep_host(inputs)
    if "full" not in _NC_CACHE:
        _NC_CACHE["full"] = build("full")
    nc = _NC_CACHE["full"]
    res = run_bass_kernel_spmd(nc, in_maps, core_ids=list(range(8)))
    full = np.empty((2, D, TB), np.float32)
    for k in range(8):
        b, qq = divmod(k, 4)
        full[b][:, qq * TQ:(qq + 1) * TQ] = res.results[k]["out"]
    return np.ascontiguousarray(full.transpose(0, 2, 1))
